# version 07
# baseline (speedup 1.0000x reference)
import os
import numpy as np
import ml_dtypes
import concourse.bass as bass
import concourse.mybir as mybir
from concourse.bass_utils import run_bass_kernel_spmd

F32 = mybir.dt.float32
BF16 = mybir.dt.bfloat16
F16 = mybir.dt.float16
AF = mybir.ActivationFunctionType
ALU = mybir.AluOpType
AX = mybir.AxisListType.X

D, DI, DS, DTR, DC, I3 = 1024, 2048, 16, 64, 4, 3072
NDT = DI // 128
NSC = NDT // 2
NDM = D // 128
NI3 = I3 // 128
MAGIC = float(1.5 * 2 ** 23)
QMAX = 32767.0
N_CORES = 8
PAIRS = [[0, 1], [2, 3], [4, 5], [6, 7]]
ALL8 = [list(range(N_CORES))]


class Ctr:
    def __init__(self, sem, step=1):
        self.sem, self.n, self.step = sem, 0, step


class Prog:
    def __init__(self):
        self.items = []

    def on(self, eng, fn):
        self.items.append((eng, fn))

    def emit(self, name, eng):
        for e, fn in self.items:
            if e == name:
                fn(eng)


def build(nc, T):
    TC = min(512, T)
    NTC = T // TC
    NG = T // 128
    NT5 = T // 512
    P = Prog()
    dram = {}

    def din(name, shape, dt):
        dram[name] = nc.dram_tensor(name, shape, dt, kind="ExternalInput").ap()
        return dram[name]

    DUMP = bool(os.environ.get("KDUMP"))

    def dint(name, shape, dt):
        kind = "ExternalOutput" if (DUMP and name in (
            "xn_dm", "xq_dm", "us_sp", "sz_sp", "hmid_sp", "h2_sp",
            "up_sp", "gate_sp", "g2_sp", "o_sp", "ffn_sp", "wqTu", "wqTg", "wqTd")) else "Internal"
        dram[name] = nc.dram_tensor(name, shape, dt, kind=kind).ap()
        return dram[name]

    x = din("x", [T, D], F32)
    ident_in = din("ident", [128, 128], F32)
    kconst_in = din("kconst", [128, 8], F32)
    din("selbc", [128, 32 * 128], F32)
    for l in range(2):
        din(f"WinT{l}", [D, 2 * DI], F32)
        din(f"WxT{l}", [DI, 96], F32)
        din(f"WdtT{l}", [DTR, DI], F32)
        din(f"WoutT{l}", [DI, D], F32)
        din(f"convw{l}", [DI, DC], F32)
        din(f"convb{l}", [DI, 1], F32)
        din(f"bdt{l}", [DI, 1], F32)
        din(f"Alog{l}", [DI, DS], F32)
        din(f"Dv{l}", [DI, 1], F32)
    din("wup", [I3, D], F32)
    din("wgate", [I3, D], F32)
    din("wdown", [D, I3], F32)
    out = nc.dram_tensor("out", [T, D], F32, kind="ExternalOutput").ap()

    xn_dm = dint("xn_dm", [D, T], BF16)
    xq_dm = dint("xq_dm", [D, T], BF16)
    us_sp = dint("us_sp", [DI, T], BF16)
    sz_sp = dint("sz_sp", [DI, T], BF16)
    hmid_sp = dint("hmid_sp", [D, T], BF16)
    h2_sp = dint("h2_sp", [D, T], BF16)
    up_sp = dint("up_sp", [I3, T], F16)
    gate_sp = dint("gate_sp", [I3, T], F16)
    g2_sp = dint("g2_sp", [I3, T], F16)
    o_sp = dint("o_sp", [I3, T], F16)
    ffn_sp = dint("ffn_sp", [D, T], BF16)
    wqTu = dint("wqTu", [D, I3], BF16)
    wqTg = dint("wqTg", [D, I3], BF16)
    wqTd = dint("wqTd", [I3, D], BF16)
    cc_i = [dint(f"cc_i{i}", [1, 1], F32) for i in range(5)]
    cc_o = [dint(f"cc_o{i}", [N_CORES, 1], F32) for i in range(5)]
    gx_i = [dint(f"gx_i{i}", [DI // 2, TC], BF16) for i in range(2)]
    gx_o = [dint(f"gx_o{i}", [DI, TC], BF16) for i in range(2)]

    from contextlib import ExitStack
    with ExitStack() as st_:
        def sbt(name, shape, dt):
            return st_.enter_context(nc.sbuf_tensor(name, shape, dt))

        ident = sbt("identb_s", [128, 128], BF16)
        identf = sbt("identf_s", [128, 128], F32)
        ones1 = sbt("ones1", [1, 128], F32)
        onesb = sbt("onesb", [1, 128], BF16)
        stat_m = sbt("stat_m", [128, NG], F32)
        stat_r = sbt("stat_r", [128, NG], F32)
        Aneg = sbt("Aneg", [128, 2 * NSC * DS], F32)
        convw = sbt("convw", [128, 2 * NDT * DC], F32)
        convb = sbt("convb", [128, 2 * NDT], F32)
        bdt = sbt("bdt", [128, 2 * NSC], F32)
        Dvt = sbt("Dvt", [128, 2 * NSC], F32)
        hstate = sbt("hstate", [128, NSC * DS], F32)
        scl = sbt("scl", [128, 16], F32)
        mx = sbt("mx", [128, 8], F32)
        statv = sbt("statv", [128, NG], F32)
        cms = sbt("cms", [128, 16], F32)
        kconst = sbt("kconst_s", [128, 8], F32)
        for ci, cvv in enumerate([1e-5, -0.5, MAGIC, -MAGIC, 65536.0, 1.0]):
            nc.const_aps.register(F32, cvv, kconst[:, ci:ci + 1])

        wbig = sbt("wbig", [128, 16384], BF16)
        selb = sbt("selb", [128, 32 * 128], BF16)
        wx_s = sbt("wx_s", [128, NDT * 96], BF16)
        wdt_s = sbt("wdt_s", [128, NSC * 128], BF16)
        bcB = sbt("bcB", [128, DS * TC], BF16)
        bcC = sbt("bcC", [128, DS * TC], BF16)
        uball = sbt("uball", [128, NSC * TC], BF16)
        zball = sbt("zball", [128, NSC * TC], BF16)
        dtball = sbt("dtball", [128, NSC * TC], BF16)
        dtuall = sbt("dtuall", [128, NSC * TC], BF16)
        yball = sbt("yball", [128, NSC * TC], F32)
        ustage = sbt("ustage", [128, NDT * 512], BF16)
        gat = sbt("gat", [128, NSC * TC], BF16)
        ubank = sbt("ubank", [128, NDT * 512], BF16)
        xdbl_b = sbt("xdbl_b", [128, T], BF16)
        ta = sbt("ta", [128, 1536], F32)
        tb = sbt("tb", [128, 1024], F32)
        tcp = sbt("tcp", [128, 1024], F32)
        xnb = sbt("xnb", [128, 1024], BF16)
        xe = [sbt(f"xe{i}", [128, 512], BF16) for i in range(2)]
        uext = sbt("uext", [128, 515], BF16)
        ub = sbt("ub", [128, 512], BF16)
        ccarry = sbt("ccarry", [128, NDT * 3], BF16)
        da = [sbt(f"da{i}", [128, TC], BF16) for i in range(2)]
        dbu = sbt("dbu", [128, TC], BF16)
        hb = sbt("hb", [128, TC], BF16)
        hcb = sbt("hcb", [128, TC], BF16)
        fa = sbt("fa", [128, 512], F16)
        fb = sbt("fb", [128, 512], F16)
        fo = sbt("fo", [128, 512], BF16)

        ps = [st_.enter_context(nc.psum_tensor(f"ps{i}", [128, 512], F32)) for i in range(4)]
        ps_ev = [0, 0, 0, 0]
        psb = [st_.enter_context(nc.psum_tensor(f"psb{i}", [128, 128], BF16)) for i in range(2)]
        psb_ev = [0, 0]

        sems = {}
        for nm, stp in [("v", 1), ("s", 1), ("t", 1), ("g", 1)]:
            sems[nm] = Ctr(st_.enter_context(nc.semaphore("sem_" + nm)), stp)
        c_v, c_s, c_t, c_g = (sems[k] for k in ["v", "s", "t", "g"])
        self_ctr = {"vector": c_v, "scalar": c_s, "tensor": c_t}

        class Pool:
            def __init__(self, tag, K):
                self.ctrs = [Ctr(st_.enter_context(nc.semaphore(f"dma_{tag}{i}")), 16) for i in range(K)]
                self.i = 0

            def take(self):
                c = self.ctrs[self.i % len(self.ctrs)]
                self.i += 1
                return c

            def all(self):
                return [(c, c.n) for c in self.ctrs if c.n > 0]

        pool_ld = Pool("ld", 6)
        pool_st = Pool("st", 6)
        pool_wg = Pool("wg", 4)

        def flat(waits):
            o = []
            for w in waits:
                if isinstance(w, list):
                    o.extend(w)
                elif w is not None:
                    o.append(w)
            return o

        def grp(eng_name, waits, ctr, body):
            waits = flat(waits)
            sc = self_ctr.get(eng_name)
            if sc is not None and sc.n > 0:
                waits = waits + [(sc, sc.n)]
            if eng_name == "gpsimd":
                if c_g.n > 0:
                    waits = waits + [(c_g, c_g.n)]
            waits = [(c, v) for (c, v) in waits if v > 0]
            dedup = {}
            for (c, v) in waits:
                dedup[id(c)] = (c, max(v, dedup.get(id(c), (c, 0))[1]))
            waits = list(dedup.values())

            def f(e, waits=waits, body=body, ctr=ctr):
                for (c, v) in waits:
                    e.wait_ge(c.sem, v)
                last = body(e)
                if ctr is not None:
                    last.then_inc(ctr.sem, ctr.step)
            P.on(eng_name, f)
            if ctr is None:
                return None
            ctr.n += ctr.step
            return ctr.n

        def dgrp(eng_name, waits, pool, dmas, slow=False):
            waits = [(c, v) for (c, v) in flat(waits) if v > 0]
            dedup = {}
            for (c, v) in waits:
                dedup[id(c)] = (c, max(v, dedup.get(id(c), (c, 0))[1]))
            waits = list(dedup.values())
            plan = []
            deps = {}
            for d in dmas:
                c = pool.take()
                plan.append((c, c.n, d))
                c.n += 16
                deps[id(c)] = (c, c.n)

            def f(e, waits=waits, plan=plan, slow=slow):
                for (c, v) in waits:
                    e.wait_ge(c.sem, v)
                for (c, prev, (o_, i_)) in plan:
                    if prev > 0:
                        e.wait_ge(c.sem, prev)
                    e.dma_start(out=o_, in_=i_, allow_slow_non_contiguous=slow).then_inc(c.sem, 16)
            P.on(eng_name, f)
            return list(deps.values())

        TR = {"st": []}

        wi = dgrp("sync", [], pool_ld, [(identf[:], ident_in[:]), (kconst[:], kconst_in[:])])
        wA = dgrp("sync", [], pool_ld, [
            (ta[:, l * NSC * DS:(l + 1) * NSC * DS].rearrange("p (a b) -> p a b", a=NSC),
             dram[f"Alog{l}"][0:DI // 2, :].rearrange("(a p) b -> p a b", p=128))
            for l in range(2)])
        wP = dgrp("sync", [], pool_ld, sum([[
            (convw[:, l * NDT * DC:(l + 1) * NDT * DC].rearrange("p (a b) -> p a b", a=NDT),
             dram[f"convw{l}"].rearrange("(a p) b -> p a b", p=128)),
            (convb[:, l * NDT:(l + 1) * NDT].rearrange("p (a b) -> p a b", b=1),
             dram[f"convb{l}"].rearrange("(a p) b -> p a b", p=128)),
            (bdt[:, l * NSC:(l + 1) * NSC].rearrange("p (a b) -> p a b", b=1),
             dram[f"bdt{l}"][0:DI // 2].rearrange("(a p) b -> p a b", p=128)),
            (Dvt[:, l * NSC:(l + 1) * NSC].rearrange("p (a b) -> p a b", b=1),
             dram[f"Dv{l}"][0:DI // 2].rearrange("(a p) b -> p a b", p=128)),
        ] for l in range(2)], []), slow=True)
        wsel = dgrp("gpsimd", [], pool_wg, [(selb[:], dram["selbc"][:])])
        sv = grp("scalar", [wA], c_s, lambda s: s.activation(
            out=Aneg[:, 0:2 * NSC * DS], in_=ta[:, 0:2 * NSC * DS], func=AF.Exp))
        init_v = grp("vector", [wi, wP, (c_s, sv)], c_v, lambda v: (
            v.tensor_copy(ident[:], identf[:]),
            v.memset(ones1[:], 1.0),
            v.memset(onesb[:], 1.0),
            v.memset(mx[:], 0.0),
            v.tensor_scalar(Aneg[:, 0:2 * NSC * DS], Aneg[:, 0:2 * NSC * DS], -1.0, None, ALU.mult))[-1])

        def spill_T(src_bf, dst_dm, g, vready):
            for o in range(NDM):
                pi = o % 2
                tv = grp("tensor", [(c_v, vready), (c_s, psb_ev[pi])], c_t,
                         lambda t, o=o, pi=pi, src_bf=src_bf: t.transpose(
                             psb[pi][:, 0:128], src_bf[:, o * 128:(o + 1) * 128], ident[:]))
                sv_ = grp("scalar", [(c_t, tv), TR["st"]], c_s,
                          lambda s, o=o, pi=pi: s.copy(xe[o % 2][:, 0:128], psb[pi][:, 0:128]))
                psb_ev[pi] = sv_
                TR["st"] = dgrp("sync", [(c_s, sv_)], pool_st,
                                [(dst_dm[o * 128:(o + 1) * 128, g * 128:(g + 1) * 128],
                                  xe[o % 2][:, 0:128])])

        def ckpt(name):
            if os.environ.get("KPHASE") == name:
                raise StopIteration

        def emit_all():
            for g in range(NG):
                wx = dgrp("sync", [(c_v, init_v if g == 0 else c_v.n), (c_s, c_s.n)], pool_ld,
                          [(ta[:, 0:D], x[g * 128:(g + 1) * 128, :])])
                sv = grp("scalar", [wx], c_s,
                         lambda s: s.activation(out=tb[:, 0:D], in_=ta[:, 0:D], func=AF.Square))
                vv = grp("vector", [(c_s, sv)], c_v, lambda v, g=g: (
                    v.reduce_sum(stat_m[:, g:g + 1], ta[:, 0:D], axis=AX),
                    v.reduce_sum(stat_r[:, g:g + 1], tb[:, 0:D], axis=AX))[-1])
            vv = grp("vector", [], c_v, lambda v: (
                v.tensor_scalar(stat_m[:, 0:NG], stat_m[:, 0:NG], 1.0 / D, None, ALU.mult),
                v.tensor_scalar(stat_r[:, 0:NG], stat_r[:, 0:NG], 1.0 / D, None, ALU.mult),
                v.tensor_tensor(tcp[:, 0:NG], stat_m[:, 0:NG], stat_m[:, 0:NG], ALU.mult),
                v.tensor_sub(stat_r[:, 0:NG], stat_r[:, 0:NG], tcp[:, 0:NG]))[-1])
            sv = grp("scalar", [(c_v, vv)], c_s, lambda s: s.activation(
                out=statv[:, 0:NG], in_=stat_r[:, 0:NG], func=AF.Ln, bias=1e-5))
            sv = grp("scalar", [], c_s, lambda s: s.activation(
                out=stat_r[:, 0:NG], in_=statv[:, 0:NG], func=AF.Exp, scale=-0.5))
            stats_done_s = sv
            for g in range(NG):
                wx = dgrp("sync", [(c_v, c_v.n), (c_s, c_s.n)], pool_ld,
                          [(ta[:, 0:D], x[g * 128:(g + 1) * 128, :])])
                vv = grp("vector", [wx, (c_s, stats_done_s)], c_v, lambda v, g=g: (
                    v.tensor_scalar(tb[:, 0:D], ta[:, 0:D], stat_m[:, g:g + 1], stat_r[:, g:g + 1],
                                    ALU.subtract, ALU.mult),
                    v.reduce_max(tcp[:, 0:1], tb[:, 0:D], axis=AX, apply_absolute_value=True))[-1])
                vv = grp("vector", [], c_v, lambda v, g=g: (
                    v.tensor_max(mx[:, 0:1], mx[:, 0:1], tcp[:, 0:1]),
                    v.tensor_copy(xnb[:], tb[:, 0:D]))[-1])
                spill_T(xnb, xn_dm, g, vv)

            def coll_max(idx, mx_col, scl_col):
                pi = 3
                tv = grp("tensor", [(c_v, c_v.n), (c_s, ps_ev[pi])], c_t,
                         lambda t, mx_col=mx_col: t.transpose(ps[pi][0:1, 0:128], mx[:, mx_col:mx_col + 1], identf[:]))
                sv = grp("scalar", [(c_t, tv)], c_s, lambda s: s.copy(ta[0:1, 0:128], ps[pi][0:1, 0:128]))
                ps_ev[pi] = sv
                vv = grp("vector", [(c_s, sv)], c_v,
                         lambda v: v.reduce_max(cms[0:1, 4:5], ta[0:1, 0:128], axis=AX))
                wd = dgrp("sync", [(c_v, vv)], pool_st, [(cc_i[idx][:], cms[0:1, 4:5])])
                gv = grp("gpsimd", [wd], c_g, lambda g, idx=idx: g.collective_compute(
                    "AllGather", ALU.bypass, ins=[cc_i[idx][:]], outs=[cc_o[idx][:]],
                    replica_groups=ALL8))
                wr = dgrp("sync", [(c_g, gv)], pool_ld, [(cms[0:1, 8:8 + N_CORES], cc_o[idx].rearrange("a b -> b a"))])
                vv = grp("vector", [wr], c_v, lambda v: v.reduce_max(
                    cms[0:1, 4:5], cms[0:1, 8:8 + N_CORES], axis=AX))
                vv = grp("vector", [], c_v, lambda v: v.tensor_scalar(
                    cms[0:1, 7:8], cms[0:1, 4:5], 1.0 / QMAX, None, ALU.mult))
                vv = grp("vector", [], c_v, lambda v: v.tensor_scalar_max(
                    cms[0:1, 5:6], cms[0:1, 7:8], 1e-12))
                vv = grp("vector", [], c_v, lambda v: v.reciprocal(
                    cms[0:1, 6:7], cms[0:1, 5:6]))
                tv = grp("tensor", [(c_v, vv), (c_s, ps_ev[pi])], c_t, lambda t: t.matmul(
                    ps[pi][:, 0:2], ones1[:, :], cms[0:1, 5:7], start=True, stop=True))
                sv = grp("scalar", [(c_t, tv)], c_s,
                         lambda s, scl_col=scl_col: s.copy(scl[:, scl_col:scl_col + 2], ps[pi][:, 0:2]))
                ps_ev[pi] = sv
                return sv

            if os.environ.get("KPHASE") == "p0":
                dgrp("gpsimd", [(c_v, c_v.n), (c_s, c_s.n)], pool_wg,
                     [(xq_dm[0:128, 0:NG], stat_m[:, 0:NG]),
                      (xq_dm[0:128, NG:2 * NG], stat_r[:, 0:NG]),
                      (xq_dm[0:128, 2 * NG:2 * NG + 8], mx[:, 0:8])])
            ckpt("p0")
            c1 = coll_max(0, 0, 0)
            if os.environ.get("KPHASE") == "c1":
                dgrp("gpsimd", [(c_v, c_v.n), (c_s, c_s.n), (c_t, c_t.n)], pool_wg,
                     [(up_sp[0:128, 0:16], scl[:, 0:16]),
                      (up_sp[0:128, 16:24], mx[:, 0:8]),
                      (up_sp[0:1, 24:40], cms[0:1, 0:16]),
                      (up_sp[0:1, 40:41], cc_i[0].rearrange("a b -> b a")),
                      (up_sp[0:1, 48:56], cc_o[0].rearrange("a b -> b a"))])
            ckpt("c1")

            for g in range(NG):
                wx = dgrp("sync", [(c_v, c_v.n), (c_s, c_s.n)], pool_ld,
                          [(ta[:, 0:D], x[g * 128:(g + 1) * 128, :])])
                vv = grp("vector", [wx, (c_s, c1)], c_v, lambda v, g=g: (
                    v.tensor_scalar(tb[:, 0:D], ta[:, 0:D], stat_m[:, g:g + 1], stat_r[:, g:g + 1],
                                    ALU.subtract, ALU.mult),
                    v.tensor_scalar(tb[:, 0:D], tb[:, 0:D], scl[:, 1:2], None, ALU.mult))[-1])
                sv = grp("scalar", [(c_v, vv)], c_s, lambda s: s.add(tb[:, 0:D], tb[:, 0:D], MAGIC))
                sv = grp("scalar", [], c_s, lambda s: s.add(tb[:, 0:D], tb[:, 0:D], -MAGIC))
                vv = grp("vector", [(c_s, sv)], c_v, lambda v: (
                    v.tensor_scalar(tb[:, 0:D], tb[:, 0:D], -QMAX - 1.0, QMAX, ALU.max, ALU.min),
                    v.tensor_scalar(tb[:, 0:D], tb[:, 0:D], scl[:, 0:1], None, ALU.mult),
                    v.tensor_copy(xnb[:], tb[:, 0:D]))[-1])
                spill_T(xnb, xq_dm, g, vv)

            def lpbq(w_ap, O, I, lhsT_sp):
                NO = O // 128
                IH = I if I <= 2048 else I // 2
                NB = IH // 32
                for ot in range(NO):
                    for ih in range(I // IH):
                        wld = dgrp("sync", [(c_v, c_v.n), (c_s, c_s.n)], pool_ld,
                                   [(ta[:, 0:IH], w_ap[ot * 128:(ot + 1) * 128, ih * IH:(ih + 1) * IH])])
                        vv = grp("vector", [wld], c_v, lambda v, IH=IH, NB=NB: v.reduce_max(
                            tcp[:, 0:NB].rearrange("p (a b) -> p a b", b=1),
                            ta[:, 0:IH].rearrange("p (a b) -> p a b", b=32),
                            axis=AX, apply_absolute_value=True))
                        vv = grp("vector", [], c_v, lambda v, NB=NB: v.tensor_scalar(
                            tcp[:, NB:2 * NB], tcp[:, 0:NB], 1.0 / 7.0, None, ALU.mult))
                        vv = grp("vector", [], c_v, lambda v, NB=NB: v.tensor_scalar_max(
                            tcp[:, 0:NB], tcp[:, NB:2 * NB], 1e-12))
                        vv = grp("vector", [], c_v, lambda v, NB=NB: v.reciprocal(
                            tb[:, 0:NB], tcp[:, 0:NB]))
                        vv = grp("vector", [], c_v, lambda v, IH=IH, NB=NB: v.tensor_tensor(
                            ta[:, 0:IH].rearrange("p (a b) -> p a b", b=32),
                            ta[:, 0:IH].rearrange("p (a b) -> p a b", b=32),
                            tb[:, 0:NB].rearrange("p (a b) -> p a b", b=1).broadcast_to([128, NB, 32]),
                            ALU.mult))
                        sv = grp("scalar", [(c_v, vv)], c_s,
                                 lambda s, IH=IH: s.add(ta[:, 0:IH], ta[:, 0:IH], MAGIC))
                        sv = grp("scalar", [], c_s,
                                 lambda s, IH=IH: s.add(ta[:, 0:IH], ta[:, 0:IH], -MAGIC))
                        vv = grp("vector", [(c_s, sv)], c_v, lambda v, IH=IH, NB=NB: (
                            v.tensor_scalar(ta[:, 0:IH], ta[:, 0:IH], -8.0, 7.0, ALU.max, ALU.min),
                            v.tensor_tensor(ta[:, 0:IH].rearrange("p (a b) -> p a b", b=32),
                                            ta[:, 0:IH].rearrange("p (a b) -> p a b", b=32),
                                            tcp[:, 0:NB].rearrange("p (a b) -> p a b", b=1).broadcast_to([128, NB, 32]),
                                            ALU.mult),
                            v.tensor_copy(ubank[:, 0:IH], ta[:, 0:IH]))[-1])
                        for it in range(IH // 128):
                            itg = ih * (IH // 128) + it
                            pi = it % 2
                            tv = grp("tensor", [(c_v, vv), (c_s, psb_ev[pi])], c_t,
                                     lambda t, it=it, pi=pi: t.transpose(psb[pi][:, 0:128],
                                                                         ubank[:, it * 128:(it + 1) * 128], ident[:]))
                            sv2 = grp("scalar", [(c_t, tv), TR["st"]], c_s,
                                      lambda s, it=it, pi=pi: s.copy(xe[it % 2][:, 0:128], psb[pi][:, 0:128]))
                            psb_ev[pi] = sv2
                            TR["st"] = dgrp("sync", [(c_s, sv2)], pool_st,
                                            [(lhsT_sp[itg * 128:(itg + 1) * 128, ot * 128:(ot + 1) * 128],
                                              xe[it % 2][:, 0:128])])
                return TR["st"]

            def mm_pass(lhsT_sp, NK, NO, src_dm, dst_sp, mx_col, wready):
                OH = 12
                for oh in range((NO + OH - 1) // OH):
                    o0, o1 = oh * OH, min((oh + 1) * OH, NO)
                    no = o1 - o0
                    wld = dgrp("gpsimd", [wready, (c_t, c_t.n)], pool_wg,
                               [(wbig[:, (k * no) * 128:((k + 1) * no) * 128],
                                 lhsT_sp[k * 128:(k + 1) * 128, o0 * 128:o1 * 128]) for k in range(NK)])
                    for tch in range(NT5):
                        mv = dgrp("sync", [(c_t, c_t.n), TR["st"]], pool_ld,
                                  [(ubank[:, k * 512:(k + 1) * 512],
                                    src_dm[k * 128:(k + 1) * 128, tch * 512:(tch + 1) * 512]) for k in range(NK)])
                        for o in range(o0, o1):
                            pi = o % 2
                            tv = grp("tensor", [mv, wld, (c_s, ps_ev[pi])], c_t,
                                     lambda t, o=o, pi=pi, no=no, o0=o0, NK=NK: [
                                         t.matmul(ps[pi][:, 0:512],
                                                  wbig[:, (k * no + o - o0) * 128:(k * no + o - o0 + 1) * 128],
                                                  ubank[:, k * 512:(k + 1) * 512],
                                                  start=(k == 0), stop=(k == NK - 1))
                                         for k in range(NK)][-1])
                            sv = grp("scalar", [(c_t, tv), TR["st"]], c_s,
                                     lambda s, pi=pi, o=o: s.copy(fa[:] if o % 2 == 0 else fb[:],
                                                                  ps[pi][:, 0:512]))
                            ps_ev[pi] = sv
                            vv = grp("vector", [(c_s, sv)], c_v, lambda v, o=o: v.reduce_max(
                                tcp[:, 2:3], (fa[:] if o % 2 == 0 else fb[:]), axis=AX,
                                apply_absolute_value=True))
                            vv = grp("vector", [], c_v, lambda v, mx_col=mx_col: v.tensor_max(
                                mx[:, mx_col:mx_col + 1], mx[:, mx_col:mx_col + 1], tcp[:, 2:3]))
                            TR["st"] = dgrp("sync", [(c_v, vv)], pool_st,
                                            [(dst_sp[o * 128:(o + 1) * 128, tch * 512:(tch + 1) * 512],
                                              fa[:] if o % 2 == 0 else fb[:])])
                return TR["st"]

            if os.environ.get("KPHASE") == "f1":
                dgrp("gpsimd", [(c_v, c_v.n), (c_s, c_s.n)], pool_wg,
                     [(up_sp[0:128, 0:16], scl[:, 0:16]),
                      (up_sp[0:128, 16:24], mx[:, 0:8])])
            ckpt("f1")
            lu = lpbq(dram["wup"], I3, D, wqTu)
            mm_pass(wqTu, NDM, NI3, xq_dm, up_sp, 1, lu)
            lg = lpbq(dram["wgate"], I3, D, wqTg)
            mm_pass(wqTg, NDM, NI3, xq_dm, gate_sp, 2, lg)

            ckpt("ffnmm")
            c2 = coll_max(1, 1, 2)
            c3 = coll_max(2, 2, 4)

            for it in range(NI3):
                for tch in range(NT5):
                    wld = dgrp("sync", [(c_s, c3), (c_v, c_v.n), TR["st"]], pool_ld,
                               [(fa[:], gate_sp[it * 128:(it + 1) * 128, tch * 512:(tch + 1) * 512])])
                    vv = grp("vector", [wld], c_v,
                             lambda v: v.tensor_scalar(ta[:, 0:512], fa[:], scl[:, 5:6], None, ALU.mult))
                    sv = grp("scalar", [(c_v, vv)], c_s, lambda s: s.add(ta[:, 0:512], ta[:, 0:512], MAGIC))
                    sv = grp("scalar", [], c_s, lambda s: s.add(ta[:, 0:512], ta[:, 0:512], -MAGIC))
                    vv = grp("vector", [(c_s, sv)], c_v, lambda v: (
                        v.tensor_scalar(ta[:, 0:512], ta[:, 0:512], -QMAX - 1.0, QMAX, ALU.max, ALU.min),
                        v.tensor_scalar(ta[:, 0:512], ta[:, 0:512], scl[:, 4:5], None, ALU.mult))[-1])
                    sv = grp("scalar", [(c_v, vv)], c_s,
                             lambda s: s.activation(out=tb[:, 0:512], in_=ta[:, 0:512], func=AF.Sigmoid))
                    sv = grp("scalar", [], c_s, lambda s: s.activation(
                        out=tb[:, 0:512], in_=tb[:, 0:512], func=AF.Copy, scale=65536.0, bias=MAGIC))
                    sv = grp("scalar", [], c_s, lambda s: s.add(tb[:, 0:512], tb[:, 0:512], -MAGIC))
                    vv = grp("vector", [(c_s, sv)], c_v, lambda v: (
                        v.tensor_scalar(tb[:, 0:512], tb[:, 0:512], 0.0, 65535.0, ALU.max, ALU.min),
                        v.tensor_scalar(tb[:, 0:512], tb[:, 0:512], 1.0 / 65536.0, None, ALU.mult),
                        v.tensor_tensor(ta[:, 0:512], ta[:, 0:512], tb[:, 0:512], ALU.mult),
                        v.reduce_max(tcp[:, 2:3], ta[:, 0:512], axis=AX, apply_absolute_value=True))[-1])
                    vv = grp("vector", [], c_v, lambda v: (
                        v.tensor_max(mx[:, 3:4], mx[:, 3:4], tcp[:, 2:3]),
                        v.tensor_copy(fb[:], ta[:, 0:512]))[-1])
                    TR["st"] = dgrp("sync", [(c_v, vv)], pool_st,
                                    [(g2_sp[it * 128:(it + 1) * 128, tch * 512:(tch + 1) * 512], fb[:])])

            ckpt("f4")
            c4 = coll_max(3, 3, 6)

            for it in range(NI3):
                for tch in range(NT5):
                    wld = dgrp("sync", [(c_s, c4), (c_v, c_v.n), TR["st"]], pool_ld,
                               [(fa[:], g2_sp[it * 128:(it + 1) * 128, tch * 512:(tch + 1) * 512]),
                                (fb[:], up_sp[it * 128:(it + 1) * 128, tch * 512:(tch + 1) * 512])])
                    vv = grp("vector", [wld], c_v, lambda v: (
                        v.tensor_scalar(ta[:, 0:512], fa[:], scl[:, 7:8], None, ALU.mult),
                        v.tensor_scalar(tb[:, 0:512], fb[:], scl[:, 3:4], None, ALU.mult))[-1])
                    sv = grp("scalar", [(c_v, vv)], c_s, lambda s: (
                        s.add(ta[:, 0:512], ta[:, 0:512], MAGIC),
                        s.add(tb[:, 0:512], tb[:, 0:512], MAGIC))[-1])
                    sv = grp("scalar", [], c_s, lambda s: (
                        s.add(ta[:, 0:512], ta[:, 0:512], -MAGIC),
                        s.add(tb[:, 0:512], tb[:, 0:512], -MAGIC))[-1])
                    vv = grp("vector", [(c_s, sv)], c_v, lambda v: (
                        v.tensor_scalar(ta[:, 0:512], ta[:, 0:512], -QMAX - 1.0, QMAX, ALU.max, ALU.min),
                        v.tensor_scalar(tb[:, 0:512], tb[:, 0:512], -QMAX - 1.0, QMAX, ALU.max, ALU.min),
                        v.tensor_scalar(ta[:, 0:512], ta[:, 0:512], scl[:, 6:7], None, ALU.mult),
                        v.tensor_scalar(tb[:, 0:512], tb[:, 0:512], scl[:, 2:3], None, ALU.mult),
                        v.tensor_tensor(ta[:, 0:512], ta[:, 0:512], tb[:, 0:512], ALU.mult),
                        v.reduce_max(tcp[:, 2:3], ta[:, 0:512], axis=AX, apply_absolute_value=True))[-1])
                    vv = grp("vector", [], c_v, lambda v: (
                        v.tensor_max(mx[:, 4:5], mx[:, 4:5], tcp[:, 2:3]),
                        v.tensor_copy(fa[:], ta[:, 0:512]))[-1])
                    TR["st"] = dgrp("sync", [(c_v, vv)], pool_st,
                                    [(o_sp[it * 128:(it + 1) * 128, tch * 512:(tch + 1) * 512], fa[:])])

            ckpt("f5")
            c5 = coll_max(4, 4, 8)

            ld_ = lpbq(dram["wdown"], D, I3, wqTd)

            def ostage(k):
                return gat[:, k * 512:(k + 1) * 512] if k < 8 else ubank[:, (k - 8) * 512:(k - 7) * 512]

            OH = 4
            for oh in range(NDM // OH):
                o0 = oh * OH
                wld = dgrp("gpsimd", [ld_, (c_t, c_t.n)], pool_wg,
                           [(wbig[:, (k * OH) * 128:((k + 1) * OH) * 128],
                             wqTd[k * 128:(k + 1) * 128, o0 * 128:(o0 + OH) * 128]) for k in range(NI3)])
                for tch in range(NT5):
                    vq = 0
                    for k in range(NI3):
                        wl = dgrp("sync", [(c_s, c5), (c_v, c_v.n), (c_t, c_t.n), TR["st"]], pool_ld,
                                  [(fa[:], o_sp[k * 128:(k + 1) * 128, tch * 512:(tch + 1) * 512])])
                        vv = grp("vector", [wl], c_v,
                                 lambda v: v.tensor_scalar(ta[:, 0:512], fa[:], scl[:, 9:10], None, ALU.mult))
                        sv = grp("scalar", [(c_v, vv)], c_s,
                                 lambda s: s.add(ta[:, 0:512], ta[:, 0:512], MAGIC))
                        sv = grp("scalar", [], c_s,
                                 lambda s: s.add(ta[:, 0:512], ta[:, 0:512], -MAGIC))
                        vq = grp("vector", [(c_s, sv)], c_v, lambda v, k=k: (
                            v.tensor_scalar(ta[:, 0:512], ta[:, 0:512], -QMAX - 1.0, QMAX, ALU.max, ALU.min),
                            v.tensor_scalar(ta[:, 0:512], ta[:, 0:512], scl[:, 8:9], None, ALU.mult),
                            v.tensor_copy(ostage(k), ta[:, 0:512]))[-1])
                    for o in range(o0, o0 + OH):
                        pi = o % 2
                        tv = grp("tensor", [(c_v, vq), wld, (c_s, ps_ev[pi])], c_t,
                                 lambda t, o=o, pi=pi, o0=o0: [
                                     t.matmul(ps[pi][:, 0:512],
                                              wbig[:, (k * OH + o - o0) * 128:(k * OH + o - o0 + 1) * 128],
                                              ostage(k), start=(k == 0), stop=(k == NI3 - 1))
                                     for k in range(NI3)][-1])
                        sv = grp("scalar", [(c_t, tv), TR["st"]], c_s,
                                 lambda s, pi=pi: s.copy(fo[:], ps[pi][:, 0:512]))
                        ps_ev[pi] = sv
                        TR["st"] = dgrp("sync", [(c_s, sv)], pool_st,
                                        [(ffn_sp[o * 128:(o + 1) * 128, tch * 512:(tch + 1) * 512], fo[:])])

            def layer(l, src_dm, dst_dm):
                vz = grp("vector", [pool_ld.all(), pool_st.all()], c_v, lambda v: (
                    v.memset(hstate[:], 0.0),
                    v.memset(ccarry[:], 0.0))[-1])
                prevstore = []
                prevxdbl_t = 0
                for part in range(2):
                    wld = dgrp("gpsimd", [(c_t, c_t.n)], pool_wg,
                               [(wbig[:, (k * NDT) * 128:((k + 1) * NDT) * 128],
                                 dram[f"WinT{l}"][k * 128:(k + 1) * 128, part * DI:(part + 1) * DI])
                                for k in range(NDM)])
                    if part == 0:
                        wldx = dgrp("gpsimd", [(c_t, c_t.n)], pool_wg,
                                    [(wx_s[:, kt * 96:(kt + 1) * 96],
                                      dram[f"WxT{l}"][kt * 128:(kt + 1) * 128, :]) for kt in range(NDT)])
                    for tch in range(NT5):
                        mv = dgrp("sync", [(c_t, c_t.n)], pool_ld,
                                  [(ubank[:, k * 512:(k + 1) * 512],
                                    src_dm[k * 128:(k + 1) * 128, tch * 512:(tch + 1) * 512]) for k in range(NDM)])
                        uvv = vz
                        for o in range(NDT):
                            pi = o % 2
                            tv = grp("tensor", [mv, wld, (c_s, ps_ev[pi])], c_t,
                                     lambda t, o=o, pi=pi: [
                                         t.matmul(ps[pi][:, 0:512],
                                                  wbig[:, (k * NDT + o) * 128:(k * NDT + o + 1) * 128],
                                                  ubank[:, k * 512:(k + 1) * 512],
                                                  start=(k == 0), stop=(k == NDM - 1))
                                         for k in range(NDM)][-1])
                            if part == 0:
                                sv = grp("scalar", [(c_t, tv), (c_v, uvv),
                                                    prevstore, (c_t, prevxdbl_t)], c_s,
                                         lambda s, pi=pi: s.copy(uext[:, 3:515], ps[pi][:, 0:512]))
                                ps_ev[pi] = sv
                                cwc = l * NDT * DC + o * DC
                                uvv = grp("vector", [(c_s, sv)], c_v, lambda v, o=o, cwc=cwc: (
                                    v.tensor_copy(uext[:, 0:3], ccarry[:, o * 3:o * 3 + 3]),
                                    v.tensor_copy(ccarry[:, o * 3:o * 3 + 3], uext[:, 512:515]),
                                    v.tensor_scalar(ub[:], uext[:, 0:512], convw[:, cwc + 0:cwc + 1], None, ALU.mult),
                                    v.scalar_tensor_tensor(ub[:], uext[:, 1:513], convw[:, cwc + 1:cwc + 2], ub[:],
                                                           ALU.mult, ALU.add),
                                    v.scalar_tensor_tensor(ub[:], uext[:, 2:514], convw[:, cwc + 2:cwc + 3], ub[:],
                                                           ALU.mult, ALU.add),
                                    v.scalar_tensor_tensor(ub[:], uext[:, 3:515], convw[:, cwc + 3:cwc + 4], ub[:],
                                                           ALU.mult, ALU.add))[-1])
                                grp("scalar", [(c_v, uvv)], c_s,
                                    lambda s, o=o, l=l: s.activation(
                                        out=ustage[:, o * 512:(o + 1) * 512], in_=ub[:], func=AF.Silu,
                                        bias=convb[:, l * NDT + o:l * NDT + o + 1]))
                            else:
                                sv = grp("scalar", [(c_t, tv), prevstore], c_s,
                                         lambda s, pi=pi, o=o: s.activation(
                                             out=ustage[:, o * 512:(o + 1) * 512], in_=ps[pi][:, 0:512],
                                             func=AF.Silu))
                                ps_ev[pi] = sv
                        stv = c_s.n
                        dst_sp = us_sp if part == 0 else sz_sp
                        prevstore = dgrp("sync", [(c_s, stv)], pool_st,
                                         [(dst_sp.rearrange("(a p) t -> p a t", p=128)[:, :, tch * 512:(tch + 1) * 512],
                                           ustage[:].rearrange("p (a t) -> p a t", a=NDT))])
                        if part == 0:
                            pi = 2
                            tv = grp("tensor", [(c_s, stv), (c_s, ps_ev[pi]), wldx], c_t,
                                     lambda t, pi=pi: [
                                         t.matmul(ps[pi][0:96, 0:512],
                                                  wx_s[:, kt * 96:(kt + 1) * 96],
                                                  ustage[:, kt * 512:(kt + 1) * 512],
                                                  start=(kt == 0), stop=(kt == NDT - 1))
                                         for kt in range(NDT)][-1])
                            prevxdbl_t = tv
                            sv = grp("scalar", [(c_t, tv)], c_s,
                                     lambda s, tch=tch, pi=pi: s.copy(xdbl_b[0:96, tch * 512:(tch + 1) * 512],
                                                                      ps[pi][0:96, 0:512]))
                            ps_ev[pi] = sv
                wldo = dgrp("gpsimd", [(c_t, c_t.n)], pool_wg,
                            [(wbig[:, (k * NDM) * 128:((k + 1) * NDM) * 128],
                              dram[f"WoutT{l}"][k * 128:(k + 1) * 128, :]) for k in range(NDT)])
                wldd = dgrp("gpsimd", [(c_t, c_t.n)], pool_wg,
                            [(wdt_s[0:64, dt * 128:(dt + 1) * 128],
                              dram[f"WdtT{l}"][:, (DI // 2) * 0 + dt * 128:dt * 128 + 128]) for dt in range(NSC)])
                pa_done_st = pool_st.all()
                prev_scan_v = 0
                prev_gx = []
                da_rd = [0, 0]
                dtb_v = {}
                for c in range(NTC):
                    cs = c * TC
                    for n in range(DS):
                        for (jsel, dstt) in ((n, bcB), (16 + n, bcC)):
                            pi = 2 + (n % 2)
                            tv = grp("tensor", [(c_s, c_s.n), (c_s, ps_ev[pi]), wsel], c_t,
                                     lambda t, jsel=jsel, pi=pi, cs=cs: t.matmul(
                                         ps[pi][:, 0:TC], selb[64:96, jsel * 128:(jsel + 1) * 128],
                                         xdbl_b[64:96, cs:cs + TC], start=True, stop=True))
                            sv = grp("scalar", [(c_t, tv), (c_v, prev_scan_v)], c_s,
                                     lambda s, n=n, dstt=dstt, pi=pi: s.copy(dstt[:, n * TC:(n + 1) * TC],
                                                                             ps[pi][:, 0:TC]))
                            ps_ev[pi] = sv
                    bc_s = c_s.n
                    for dt in range(NSC):
                        mvu = dgrp("sync", [(c_v, prev_scan_v), (pa_done_st if c == 0 else None)], pool_ld,
                                   [(uball[:, dt * TC:(dt + 1) * TC], us_sp[dt * 128:(dt + 1) * 128, cs:cs + TC]),
                                    (zball[:, dt * TC:(dt + 1) * TC], sz_sp[dt * 128:(dt + 1) * 128, cs:cs + TC])])
                        pi = 2
                        tv = grp("tensor", [(c_s, ps_ev[pi]), wldd], c_t,
                                 lambda t, dt=dt, pi=pi, cs=cs: t.matmul(
                                     ps[pi][:, 0:TC], wdt_s[0:64, dt * 128:(dt + 1) * 128],
                                     xdbl_b[0:64, cs:cs + TC], start=True, stop=True))
                        sv = grp("scalar", [(c_t, tv), (c_v, prev_scan_v),
                                            (c_v, dtb_v.get(dt - 1, 0))], c_s,
                                 lambda s, dt=dt, pi=pi, l=l: s.activation(
                                     out=ta[:, 0:TC], in_=ps[pi][:, 0:TC],
                                     func=AF.Identity, bias=bdt[:, l * NSC + dt:l * NSC + dt + 1]))
                        ps_ev[pi] = sv
                        sv = grp("scalar", [(c_s, sv)], c_s,
                                 lambda s: s.activation(out=tb[:, 0:TC], in_=ta[:, 0:TC], func=AF.Abs))
                        sv = grp("scalar", [], c_s,
                                 lambda s: s.activation(out=tb[:, 0:TC], in_=tb[:, 0:TC], func=AF.Exp, scale=-1.0))
                        sv = grp("scalar", [], c_s,
                                 lambda s: s.activation(out=tb[:, 0:TC], in_=tb[:, 0:TC], func=AF.Ln, bias=1.0))
                        dtb_v[dt] = grp("vector", [(c_s, sv), mvu], c_v, lambda v, dt=dt: (
                            v.tensor_scalar_max(ta[:, 0:TC], ta[:, 0:TC], 0.0),
                            v.tensor_tensor(dtball[:, dt * TC:(dt + 1) * TC], ta[:, 0:TC], tb[:, 0:TC], ALU.add),
                            v.tensor_tensor(dtuall[:, dt * TC:(dt + 1) * TC],
                                            dtball[:, dt * TC:(dt + 1) * TC],
                                            uball[:, dt * TC:(dt + 1) * TC], ALU.mult))[-1])
                    for dt in range(NSC):
                        for n in range(DS):
                            pp = (dt * DS + n) % 2
                            sv = grp("scalar", [(c_s, bc_s), (c_v, da_rd[pp]), (c_v, dtb_v[dt])], c_s,
                                     lambda s, dt=dt, n=n, pp=pp, l=l: s.activation(
                                         out=da[pp][:], in_=dtball[:, dt * TC:(dt + 1) * TC], func=AF.Exp,
                                         scale=Aneg[:, l * NSC * DS + dt * DS + n:l * NSC * DS + dt * DS + n + 1]))
                            hs = dt * DS + n
                            if os.environ.get("KDBG") and l == 0 and c == 0 and dt == 0 and n == 0:
                                dbg_v = grp("vector", [(c_s, sv)], c_v, lambda v: (
                                    v.tensor_tensor(dbu[:], dtuall[:, 0:TC], bcB[:, 0:TC], ALU.mult),
                                    v.tensor_copy(ta[:, 0:TC], da[pp][:]),
                                    v.tensor_copy(tb[:, 0:TC], dbu[:]),
                                    v.tensor_copy(tcp[:, 0:TC // 2], dtball[:, 0:TC // 2]))[-1])
                                st1 = dgrp("sync", [(c_v, dbg_v)], pool_st, [
                                    (out[0:128, 0:TC], ta[:, 0:TC]),
                                    (out[128:256, 0:TC], tb[:, 0:TC]),
                                    (out[256:384, 0:TC // 2], tcp[:, 0:TC // 2])])
                                dbg_v2 = grp("vector", [(c_v, dbg_v), st1], c_v, lambda v: (
                                    v.tensor_copy(ta[:, 0:TC], bcB[:, 0:TC]),
                                    v.tensor_copy(tb[:, 0:TC], uball[:, 0:TC]),
                                    v.tensor_copy(tcp[0:96, 0:512], xdbl_b[0:96, 0:512]))[-1])
                                dgrp("sync", [(c_v, dbg_v2)], pool_st, [
                                    (out[384:512, 0:TC], ta[:, 0:TC]),
                                    (out[256:384, TC // 2:TC // 2 + TC] if False else out[128 + 256:128 + 384, 0:0 + TC], tb[:, 0:TC])])
                                raise StopIteration
                            da_rd[pp] = grp("vector", [(c_s, sv)], c_v, lambda v, dt=dt, n=n, pp=pp, hs=hs: (
                                v.tensor_tensor(dbu[:], dtuall[:, dt * TC:(dt + 1) * TC],
                                                bcB[:, n * TC:(n + 1) * TC], ALU.mult),
                                v.tensor_tensor_scan(hb[:], da[pp][:], dbu[:],
                                                     hstate[:, hs:hs + 1], ALU.mult, ALU.add),
                                v.tensor_copy(hstate[:, hs:hs + 1], hb[:, TC - 1:TC]),
                                v.tensor_tensor(hcb[:], hb[:], bcC[:, n * TC:(n + 1) * TC], ALU.mult),
                                (v.tensor_copy(yball[:, dt * TC:(dt + 1) * TC], hcb[:]) if n == 0 else
                                 v.tensor_tensor(yball[:, dt * TC:(dt + 1) * TC],
                                                 yball[:, dt * TC:(dt + 1) * TC], hcb[:], ALU.add)))[-1])
                        grp("vector", [prev_gx], c_v, lambda v, dt=dt, l=l: (
                            v.scalar_tensor_tensor(yball[:, dt * TC:(dt + 1) * TC],
                                                   uball[:, dt * TC:(dt + 1) * TC],
                                                   Dvt[:, l * NSC + dt:l * NSC + dt + 1],
                                                   yball[:, dt * TC:(dt + 1) * TC], ALU.mult, ALU.add),
                            v.tensor_tensor(gat[:, dt * TC:(dt + 1) * TC], yball[:, dt * TC:(dt + 1) * TC],
                                            zball[:, dt * TC:(dt + 1) * TC], ALU.mult))[-1])
                    prev_scan_v = c_v.n
                    prev_gx = dgrp("sync", [(c_v, prev_scan_v), (c_g, c_g.n)], pool_st,
                                   [(gx_i[c % 2][:].rearrange("(a p) t -> p a t", p=128),
                                     gat[:].rearrange("p (a t) -> p a t", a=NSC))])
                    gv = grp("gpsimd", [prev_gx], c_g, lambda g, c=c: g.collective_compute(
                        "AllGather", ALU.bypass, ins=[gx_i[c % 2][:]], outs=[gx_o[c % 2][:]],
                        replica_groups=PAIRS))
                    for sub in range(TC // 512):
                        mvw = dgrp("sync", [(c_g, gv), (c_t, c_t.n)], pool_ld,
                                   [(ubank[:, k * 512:(k + 1) * 512],
                                     gx_o[c % 2][k * 128:(k + 1) * 128, sub * 512:(sub + 1) * 512])
                                    for k in range(NDT)])
                        for o in range(NDM):
                            pi = o % 2
                            tv = grp("tensor", [mvw, wldo, (c_s, ps_ev[pi])], c_t,
                                     lambda t, o=o, pi=pi: [
                                         t.matmul(ps[pi][:, 0:512],
                                                  wbig[:, (k * NDM + o) * 128:(k * NDM + o + 1) * 128],
                                                  ubank[:, k * 512:(k + 1) * 512],
                                                  start=(k == 0), stop=(k == NDT - 1))
                                         for k in range(NDT)][-1])
                            sv = grp("scalar", [(c_t, tv), TR["st"]], c_s,
                                     lambda s, pi=pi: s.copy(fo[:], ps[pi][:, 0:512]))
                            ps_ev[pi] = sv
                            TR["st"] = dgrp("sync", [(c_s, sv)], pool_st,
                                            [(dst_dm[o * 128:(o + 1) * 128, cs + sub * 512:cs + (sub + 1) * 512],
                                              fo[:])])

            ckpt("f6")
            try:
                layer(0, xn_dm, hmid_sp)
                layer(1, hmid_sp, h2_sp)
            except StopIteration:
                pass

            SKIP_ASM = bool(os.environ.get("KDBG"))
            outst = []
            for tch in range([] if SKIP_ASM else range(NT5)) if False else (range(0) if SKIP_ASM else range(NT5)):
                mv = dgrp("sync", [(c_v, c_v.n), TR["st"]], pool_ld,
                          [(uball[:, k * 512:(k + 1) * 512],
                            h2_sp[k * 128:(k + 1) * 128, tch * 512:(tch + 1) * 512]) for k in range(NDM)] +
                          [(zball[:, k * 512:(k + 1) * 512],
                            ffn_sp[k * 128:(k + 1) * 128, tch * 512:(tch + 1) * 512]) for k in range(NDM)])
                vv = grp("vector", [mv], c_v, lambda v: [
                    v.tensor_tensor(yball[:, k * 512:(k + 1) * 512], uball[:, k * 512:(k + 1) * 512],
                                    zball[:, k * 512:(k + 1) * 512], ALU.add) for k in range(NDM)][-1])
                for g4 in range(4):
                    g = tch * 4 + g4
                    for o in range(NDM):
                        pi = 2 + (o % 2)
                        tv = grp("tensor", [(c_v, vv), (c_s, ps_ev[pi])], c_t,
                                 lambda t, o=o, g4=g4, pi=pi: t.transpose(
                                     ps[pi][:, 0:128], yball[:, o * 512 + g4 * 128:o * 512 + (g4 + 1) * 128],
                                     identf[:]))
                        sv = grp("scalar", [(c_t, tv), outst], c_s,
                                 lambda s, o=o, pi=pi: s.copy(tb[:, o * 128:(o + 1) * 128], ps[pi][:, 0:128]))
                        ps_ev[pi] = sv
                    wx = dgrp("sync", [(c_v, c_v.n)], pool_ld, [(ta[:, 0:D], x[g * 128:(g + 1) * 128, :])])
                    vv2 = grp("vector", [wx, (c_s, c_s.n)], c_v, lambda v, g=g: (
                        v.tensor_scalar(tcp[:, 0:D], ta[:, 0:D],
                                        stat_m[:, g:g + 1], stat_r[:, g:g + 1], ALU.subtract, ALU.mult),
                        v.tensor_tensor(tb[:, 0:D], tb[:, 0:D], tcp[:, 0:D], ALU.add),
                        v.tensor_tensor(tb[:, 0:D], tb[:, 0:D], ta[:, 0:D], ALU.add))[-1])
                    outst = dgrp("sync", [(c_v, vv2)], pool_st, [(out[g * 128:(g + 1) * 128, :], tb[:, 0:D])])

        try:
            emit_all()
        except StopIteration:
            pass

        fin_deps = pool_ld.all() + pool_st.all() + pool_wg.all()

        def f_fin(e, fin_deps=fin_deps):
            for (c, v) in fin_deps:
                e.wait_ge(c.sem, v)
        P.on("gpsimd", f_fin)

        with nc.Block() as blk:
            @blk.sync
            def _(sy):
                P.emit("sync", sy)

            @blk.vector
            def _(v):
                P.emit("vector", v)

            @blk.scalar
            def _(s):
                P.emit("scalar", s)

            @blk.gpsimd
            def _(g):
                P.emit("gpsimd", g)

            @blk.tensor
            def _(t):
                P.emit("tensor", t)

    return dram


def make_in_maps(inputs, T=4096):
    xf = np.ascontiguousarray(inputs["L0_attn_x"])
    ident = np.eye(128, dtype=np.float32)
    kconst = np.tile(np.array([1e-5, -0.5, MAGIC, -MAGIC, 65536.0, 1.0, 0, 0], np.float32), (128, 1))
    selbc = np.zeros((128, 32 * 128), np.float32)
    for jj in range(32):
        selbc[64 + jj, jj * 128:(jj + 1) * 128] = 1.0
    in_maps = []
    for c in range(N_CORES):
        b, j = c // 2, c % 2
        perm = np.arange(DI) if j == 0 else np.r_[DI // 2:DI, 0:DI // 2]
        im = {"x": np.ascontiguousarray(xf[b][:T]), "ident": ident, "kconst": kconst, "selbc": selbc}
        for l in range(2):
            Wi = inputs["L0_attn_W_in"][l]
            Wu, Wz = Wi[:DI][perm], Wi[DI:][perm]
            im[f"WinT{l}"] = np.ascontiguousarray(np.concatenate([Wu, Wz], 0).T)
            im[f"WxT{l}"] = np.ascontiguousarray(inputs["L0_attn_W_x"][l][:, perm].T)
            im[f"WdtT{l}"] = np.ascontiguousarray(inputs["L0_attn_W_dt"][l][perm].T)
            im[f"WoutT{l}"] = np.ascontiguousarray(inputs["L0_attn_W_out"][l].T)
            im[f"convw{l}"] = np.ascontiguousarray(inputs["L0_attn_conv_w"][l][perm])
            im[f"convb{l}"] = np.ascontiguousarray(inputs["L0_attn_conv_b"][l][perm][:, None])
            im[f"bdt{l}"] = np.ascontiguousarray(inputs["L0_attn_b_dt"][l][perm][:, None])
            im[f"Alog{l}"] = np.ascontiguousarray(inputs["L0_attn_A_log"][l][perm])
            im[f"Dv{l}"] = np.ascontiguousarray(inputs["L0_attn_D"][l][perm][:, None])
        im["wup"] = np.ascontiguousarray(inputs["L0_ffn_w_up"])
        im["wgate"] = np.ascontiguousarray(inputs["L0_ffn_w_gate"])
        im["wdown"] = np.ascontiguousarray(inputs["L0_ffn_w_down"])
        in_maps.append(im)
    return in_maps


def kernel(**inputs):
    T = 4096
    nc = bass.Bass("TRN2", target_bir_lowering=False, detect_race_conditions=False)
    build(nc, T)
    in_maps = make_in_maps(inputs, T)
    res = run_bass_kernel_spmd(nc, in_maps, core_ids=list(range(N_CORES)))
    out = np.stack([res.results[2 * b]["out"] for b in range(4)], 0)
    return out.astype(np.float32)


# version 08
# speedup vs baseline: 1.2274x; 1.2274x over previous
import os
import numpy as np
import ml_dtypes
import concourse.bass as bass
import concourse.mybir as mybir
from concourse.bass_utils import run_bass_kernel_spmd

F32 = mybir.dt.float32
BF16 = mybir.dt.bfloat16
F16 = mybir.dt.float16
AF = mybir.ActivationFunctionType
ALU = mybir.AluOpType
AX = mybir.AxisListType.X

D, DI, DS, DTR, DC, I3 = 1024, 2048, 16, 64, 4, 3072
NDT = DI // 128
NSC = NDT // 2
NDM = D // 128
NI3 = I3 // 128
MAGIC = float(1.5 * 2 ** 23)
QMAX = 32767.0
N_CORES = 8
PAIRS = [[0, 1], [2, 3], [4, 5], [6, 7]]
ALL8 = [list(range(N_CORES))]


class Ctr:
    def __init__(self, sem, step=1):
        self.sem, self.n, self.step = sem, 0, step


class Prog:
    def __init__(self):
        self.items = []
        self.phase = "init"

    def on(self, eng, fn):
        self.items.append((eng, fn, self.phase))

    def emit(self, name, eng, nc=None):
        for e, fn, ph in self.items:
            if e == name:
                if nc is not None:
                    with nc.named_scope(ph):
                        fn(eng)
                else:
                    fn(eng)


def build(nc, T):
    TC = min(512, T)
    NTC = T // TC
    NG = T // 128
    NT5 = T // 512
    P = Prog()
    dram = {}

    def din(name, shape, dt):
        dram[name] = nc.dram_tensor(name, shape, dt, kind="ExternalInput").ap()
        return dram[name]

    DUMP = bool(os.environ.get("KDUMP"))

    def dint(name, shape, dt):
        kind = "ExternalOutput" if (DUMP and name in (
            "xn_dm", "xq_dm", "us_sp", "sz_sp", "hmid_sp", "h2_sp",
            "up_sp", "gate_sp", "g2_sp", "o_sp", "ffn_sp", "wqTu", "wqTg", "wqTd")) else "Internal"
        dram[name] = nc.dram_tensor(name, shape, dt, kind=kind).ap()
        return dram[name]

    x = din("x", [T, D], F32)
    ident_in = din("ident", [128, 128], F32)
    kconst_in = din("kconst", [128, 8], F32)
    din("selbc", [128, 32 * 128], F32)
    for l in range(2):
        din(f"WinT{l}", [D, 2 * DI], F32)
        din(f"WxT{l}", [DI, 96], F32)
        din(f"WdtT{l}", [DTR, DI], F32)
        din(f"WoutT{l}", [DI, D], F32)
        din(f"convw{l}", [DI, DC], F32)
        din(f"convb{l}", [DI, 1], F32)
        din(f"bdt{l}", [DI, 1], F32)
        din(f"Alog{l}", [DI, DS], F32)
        din(f"Dv{l}", [DI, 1], F32)
    din("wup", [I3, D], F32)
    din("wgate", [I3, D], F32)
    din("wdown", [D, I3], F32)
    out = nc.dram_tensor("out", [T, D], F32, kind="ExternalOutput").ap()

    xn_dm = dint("xn_dm", [D, T], BF16)
    xq_dm = dint("xq_dm", [D, T], BF16)
    us_sp = dint("us_sp", [DI, T], BF16)
    sz_sp = dint("sz_sp", [DI, T], BF16)
    hmid_sp = dint("hmid_sp", [D, T], BF16)
    h2_sp = dint("h2_sp", [D, T], BF16)
    up_sp = dint("up_sp", [I3, T], F16)
    gate_sp = dint("gate_sp", [I3, T], F16)
    g2_sp = dint("g2_sp", [I3, T], F16)
    o_sp = dint("o_sp", [I3, T], F16)
    ffn_sp = dint("ffn_sp", [D, T], BF16)
    wqTu = dint("wqTu", [D, I3], BF16)
    wqTg = dint("wqTg", [D, I3], BF16)
    wqTd = dint("wqTd", [I3, D], BF16)
    cc_i = [dint(f"cc_i{i}", [1, 1], F32) for i in range(5)]
    cc_o = [dint(f"cc_o{i}", [N_CORES, 1], F32) for i in range(5)]
    gx_i = [dint(f"gx_i{i}", [DI // 2, TC], BF16) for i in range(2)]
    gx_o = [dint(f"gx_o{i}", [DI, TC], BF16) for i in range(2)]

    from contextlib import ExitStack
    with ExitStack() as st_:
        def sbt(name, shape, dt):
            return st_.enter_context(nc.sbuf_tensor(name, shape, dt))

        ident = sbt("identb_s", [128, 128], BF16)
        identf = sbt("identf_s", [128, 128], F32)
        ones1 = sbt("ones1", [1, 128], F32)
        onesb = sbt("onesb", [1, 128], BF16)
        stat_m = sbt("stat_m", [128, NG], F32)
        stat_r = sbt("stat_r", [128, NG], F32)
        Aneg = sbt("Aneg", [128, 2 * NSC * DS], F32)
        convw = sbt("convw", [128, 2 * NDT * DC], F32)
        convb = sbt("convb", [128, 2 * NDT], F32)
        bdt = sbt("bdt", [128, 2 * NSC], F32)
        Dvt = sbt("Dvt", [128, 2 * NSC], F32)
        hstate = sbt("hstate", [128, NSC * DS], F32)
        scl = sbt("scl", [128, 16], F32)
        mx = sbt("mx", [128, 8], F32)
        statv = sbt("statv", [128, NG], F32)
        cms = sbt("cms", [128, 16], F32)
        kconst = sbt("kconst_s", [128, 8], F32)
        for ci, cvv in enumerate([1e-5, -0.5, MAGIC, -MAGIC, 65536.0, 1.0]):
            nc.const_aps.register(F32, cvv, kconst[:, ci:ci + 1])

        wbig = sbt("wbig", [128, 16384], BF16)
        selb = sbt("selb", [128, 32 * 128], BF16)
        wx_s = sbt("wx_s", [128, NDT * 96], BF16)
        wdt_s = sbt("wdt_s", [128, NSC * 128], BF16)
        bcB = sbt("bcB", [128, DS * TC], BF16)
        bcC = sbt("bcC", [128, DS * TC], BF16)
        uball = sbt("uball", [128, NSC * TC], BF16)
        zball = sbt("zball", [128, NSC * TC], BF16)
        dtball = sbt("dtball", [128, NSC * TC], BF16)
        dtuall = sbt("dtuall", [128, NSC * TC], BF16)
        yball = sbt("yball", [128, NSC * TC], F32)
        ustage = sbt("ustage", [128, NDT * 512], BF16)
        gat = sbt("gat", [128, NSC * TC], BF16)
        ubank = sbt("ubank", [128, NDT * 512], BF16)
        xdbl_b = sbt("xdbl_b", [128, T], BF16)
        ta = sbt("ta", [128, 1536], F32)
        tb = sbt("tb", [128, 1024], F32)
        tcp = sbt("tcp", [128, 1024], F32)
        xnb = sbt("xnb", [128, 1024], BF16)
        xe = [sbt(f"xe{i}", [128, 512], BF16) for i in range(2)]
        uext = sbt("uext", [128, 515], BF16)
        ub = sbt("ub", [128, 512], BF16)
        ccarry = sbt("ccarry", [128, NDT * 3], BF16)
        da = [sbt(f"da{i}", [128, TC], BF16) for i in range(2)]
        dbu = sbt("dbu", [128, TC], BF16)
        hb = sbt("hb", [128, TC], BF16)
        hcb = sbt("hcb", [128, TC], BF16)
        fa = sbt("fa", [128, 512], F16)
        fb = sbt("fb", [128, 512], F16)
        fa2 = sbt("fa2", [128, 512], F16)
        fb2 = sbt("fb2", [128, 512], F16)
        fo = sbt("fo", [128, 512], BF16)

        ps = [st_.enter_context(nc.psum_tensor(f"ps{i}", [128, 512], F32)) for i in range(4)]
        ps_ev = [0, 0, 0, 0]
        psb = [st_.enter_context(nc.psum_tensor(f"psb{i}", [128, 128], BF16)) for i in range(2)]
        psb_ev = [0, 0]

        sems = {}
        for nm, stp in [("v", 1), ("s", 1), ("t", 1), ("g", 1)]:
            sems[nm] = Ctr(st_.enter_context(nc.semaphore("sem_" + nm)), stp)
        c_v, c_s, c_t, c_g = (sems[k] for k in ["v", "s", "t", "g"])
        self_ctr = {"vector": c_v, "scalar": c_s, "tensor": c_t}

        class Pool:
            def __init__(self, tag, K):
                self.ctrs = [Ctr(st_.enter_context(nc.semaphore(f"dma_{tag}{i}")), 16) for i in range(K)]
                self.i = 0

            def take(self):
                c = self.ctrs[self.i % len(self.ctrs)]
                self.i += 1
                return c

            def all(self):
                return [(c, c.n) for c in self.ctrs if c.n > 0]

        pool_ld = Pool("ld", 6)
        pool_st = Pool("st", 6)
        pool_wg = Pool("wg", 4)

        def flat(waits):
            o = []
            for w in waits:
                if isinstance(w, list):
                    o.extend(w)
                elif w is not None:
                    o.append(w)
            return o

        def grp(eng_name, waits, ctr, body):
            waits = flat(waits)
            sc = self_ctr.get(eng_name)
            if sc is not None and sc.n > 0:
                waits = waits + [(sc, sc.n)]
            if eng_name == "gpsimd":
                if c_g.n > 0:
                    waits = waits + [(c_g, c_g.n)]
            waits = [(c, v) for (c, v) in waits if v > 0]
            dedup = {}
            for (c, v) in waits:
                dedup[id(c)] = (c, max(v, dedup.get(id(c), (c, 0))[1]))
            waits = list(dedup.values())

            def f(e, waits=waits, body=body, ctr=ctr):
                for (c, v) in waits:
                    e.wait_ge(c.sem, v)
                last = body(e)
                if ctr is not None:
                    last.then_inc(ctr.sem, ctr.step)
            P.on(eng_name, f)
            if ctr is None:
                return None
            ctr.n += ctr.step
            return ctr.n

        def dgrp(eng_name, waits, pool, dmas, slow=False):
            waits = [(c, v) for (c, v) in flat(waits) if v > 0]
            dedup = {}
            for (c, v) in waits:
                dedup[id(c)] = (c, max(v, dedup.get(id(c), (c, 0))[1]))
            waits = list(dedup.values())
            plan = []
            deps = {}
            for d in dmas:
                c = pool.take()
                plan.append((c, c.n, d))
                c.n += 16
                deps[id(c)] = (c, c.n)

            def f(e, waits=waits, plan=plan, slow=slow):
                for (c, v) in waits:
                    e.wait_ge(c.sem, v)
                for (c, prev, (o_, i_)) in plan:
                    if prev > 0:
                        e.wait_ge(c.sem, prev)
                    e.dma_start(out=o_, in_=i_, allow_slow_non_contiguous=slow).then_inc(c.sem, 16)
            P.on(eng_name, f)
            return list(deps.values())

        TR = {"st": []}

        wi = dgrp("sync", [], pool_ld, [(identf[:], ident_in[:]), (kconst[:], kconst_in[:])])
        wA = dgrp("sync", [], pool_ld, [
            (ta[:, l * NSC * DS:(l + 1) * NSC * DS].rearrange("p (a b) -> p a b", a=NSC),
             dram[f"Alog{l}"][0:DI // 2, :].rearrange("(a p) b -> p a b", p=128))
            for l in range(2)])
        wP = dgrp("sync", [], pool_ld, sum([[
            (convw[:, l * NDT * DC:(l + 1) * NDT * DC].rearrange("p (a b) -> p a b", a=NDT),
             dram[f"convw{l}"].rearrange("(a p) b -> p a b", p=128)),
            (convb[:, l * NDT:(l + 1) * NDT].rearrange("p (a b) -> p a b", b=1),
             dram[f"convb{l}"].rearrange("(a p) b -> p a b", p=128)),
            (bdt[:, l * NSC:(l + 1) * NSC].rearrange("p (a b) -> p a b", b=1),
             dram[f"bdt{l}"][0:DI // 2].rearrange("(a p) b -> p a b", p=128)),
            (Dvt[:, l * NSC:(l + 1) * NSC].rearrange("p (a b) -> p a b", b=1),
             dram[f"Dv{l}"][0:DI // 2].rearrange("(a p) b -> p a b", p=128)),
        ] for l in range(2)], []), slow=True)
        wsel = dgrp("gpsimd", [], pool_wg, [(selb[:], dram["selbc"][:])])
        sv = grp("scalar", [wA], c_s, lambda s: s.activation(
            out=Aneg[:, 0:2 * NSC * DS], in_=ta[:, 0:2 * NSC * DS], func=AF.Exp))
        init_v = grp("vector", [wi, wP, (c_s, sv)], c_v, lambda v: (
            v.tensor_copy(ident[:], identf[:]),
            v.memset(ones1[:], 1.0),
            v.memset(onesb[:], 1.0),
            v.memset(mx[:], 0.0),
            v.tensor_scalar(Aneg[:, 0:2 * NSC * DS], Aneg[:, 0:2 * NSC * DS], -1.0, None, ALU.mult))[-1])

        SPT = {0: [], 1: []}
        def spill_T(src_bf, dst_dm, g, vready):
            par = g % 2
            off = par * 1024
            sv_ = None
            for o in range(NDM):
                pi = o % 2
                tv = grp("tensor", [(c_v, vready), (c_s, psb_ev[pi])], c_t,
                         lambda t, o=o, pi=pi, src_bf=src_bf: t.transpose(
                             psb[pi][:, 0:128], src_bf[:, o * 128:(o + 1) * 128], ident[:]))
                sv_ = grp("scalar", [(c_t, tv), (SPT[par] if o == 0 else [])], c_s,
                          lambda s, o=o, pi=pi, off=off: s.copy(
                              ustage[:, off + o * 128:off + (o + 1) * 128], psb[pi][:, 0:128]))
                psb_ev[pi] = sv_
            SPT[par] = dgrp("sync", [(c_s, sv_)], pool_st,
                            [(dst_dm[0:D, g * 128:(g + 1) * 128].rearrange("(a p) b -> p a b", p=128),
                              ustage[:, off:off + 1024].rearrange("p (a b) -> p a b", b=128))])

        def ckpt(name):
            if os.environ.get("KPHASE") == name:
                raise StopIteration

        def emit_all():
            P.phase = "p0"
            for g in range(NG):
                wx = dgrp("sync", [(c_v, init_v if g == 0 else c_v.n), (c_s, c_s.n)], pool_ld,
                          [(ta[:, 0:D], x[g * 128:(g + 1) * 128, :])])
                sv = grp("scalar", [wx], c_s,
                         lambda s: s.activation(out=tb[:, 0:D], in_=ta[:, 0:D], func=AF.Square))
                vv = grp("vector", [(c_s, sv)], c_v, lambda v, g=g: (
                    v.reduce_sum(stat_m[:, g:g + 1], ta[:, 0:D], axis=AX),
                    v.reduce_sum(stat_r[:, g:g + 1], tb[:, 0:D], axis=AX))[-1])
            vv = grp("vector", [], c_v, lambda v: (
                v.tensor_scalar(stat_m[:, 0:NG], stat_m[:, 0:NG], 1.0 / D, None, ALU.mult),
                v.tensor_scalar(stat_r[:, 0:NG], stat_r[:, 0:NG], 1.0 / D, None, ALU.mult),
                v.tensor_tensor(tcp[:, 0:NG], stat_m[:, 0:NG], stat_m[:, 0:NG], ALU.mult),
                v.tensor_sub(stat_r[:, 0:NG], stat_r[:, 0:NG], tcp[:, 0:NG]))[-1])
            sv = grp("scalar", [(c_v, vv)], c_s, lambda s: s.activation(
                out=statv[:, 0:NG], in_=stat_r[:, 0:NG], func=AF.Ln, bias=1e-5))
            sv = grp("scalar", [], c_s, lambda s: s.activation(
                out=stat_r[:, 0:NG], in_=statv[:, 0:NG], func=AF.Exp, scale=-0.5))
            stats_done_s = sv
            for g in range(NG):
                wx = dgrp("sync", [(c_v, c_v.n), (c_s, c_s.n)], pool_ld,
                          [(ta[:, 0:D], x[g * 128:(g + 1) * 128, :])])
                vv = grp("vector", [wx, (c_s, stats_done_s)], c_v, lambda v, g=g: (
                    v.tensor_scalar(tb[:, 0:D], ta[:, 0:D], stat_m[:, g:g + 1], stat_r[:, g:g + 1],
                                    ALU.subtract, ALU.mult),
                    v.reduce_max(tcp[:, 0:1], tb[:, 0:D], axis=AX, apply_absolute_value=True))[-1])
                vv = grp("vector", [], c_v, lambda v, g=g: (
                    v.tensor_max(mx[:, 0:1], mx[:, 0:1], tcp[:, 0:1]),
                    v.tensor_copy(xnb[:], tb[:, 0:D]))[-1])
                spill_T(xnb, xn_dm, g, vv)

            def coll_max(idx, mx_col, scl_col):
                pi = 3
                tv = grp("tensor", [(c_v, c_v.n), (c_s, ps_ev[pi])], c_t,
                         lambda t, mx_col=mx_col: t.transpose(ps[pi][0:1, 0:128], mx[:, mx_col:mx_col + 1], identf[:]))
                sv = grp("scalar", [(c_t, tv)], c_s, lambda s: s.copy(ta[0:1, 0:128], ps[pi][0:1, 0:128]))
                ps_ev[pi] = sv
                vv = grp("vector", [(c_s, sv)], c_v,
                         lambda v: v.reduce_max(cms[0:1, 4:5], ta[0:1, 0:128], axis=AX))
                wd = dgrp("sync", [(c_v, vv)], pool_st, [(cc_i[idx][:], cms[0:1, 4:5])])
                gv = grp("gpsimd", [wd], c_g, lambda g, idx=idx: g.collective_compute(
                    "AllGather", ALU.bypass, ins=[cc_i[idx][:]], outs=[cc_o[idx][:]],
                    replica_groups=ALL8))
                wr = dgrp("sync", [(c_g, gv)], pool_ld, [(cms[0:1, 8:8 + N_CORES], cc_o[idx].rearrange("a b -> b a"))])
                vv = grp("vector", [wr], c_v, lambda v: v.reduce_max(
                    cms[0:1, 4:5], cms[0:1, 8:8 + N_CORES], axis=AX))
                vv = grp("vector", [], c_v, lambda v: v.tensor_scalar(
                    cms[0:1, 7:8], cms[0:1, 4:5], 1.0 / QMAX, None, ALU.mult))
                vv = grp("vector", [], c_v, lambda v: v.tensor_scalar_max(
                    cms[0:1, 5:6], cms[0:1, 7:8], 1e-12))
                vv = grp("vector", [], c_v, lambda v: v.reciprocal(
                    cms[0:1, 6:7], cms[0:1, 5:6]))
                tv = grp("tensor", [(c_v, vv), (c_s, ps_ev[pi])], c_t, lambda t: t.matmul(
                    ps[pi][:, 0:2], ones1[:, :], cms[0:1, 5:7], start=True, stop=True))
                sv = grp("scalar", [(c_t, tv)], c_s,
                         lambda s, scl_col=scl_col: s.copy(scl[:, scl_col:scl_col + 2], ps[pi][:, 0:2]))
                ps_ev[pi] = sv
                return sv

            if os.environ.get("KPHASE") == "p0":
                dgrp("gpsimd", [(c_v, c_v.n), (c_s, c_s.n)], pool_wg,
                     [(xq_dm[0:128, 0:NG], stat_m[:, 0:NG]),
                      (xq_dm[0:128, NG:2 * NG], stat_r[:, 0:NG]),
                      (xq_dm[0:128, 2 * NG:2 * NG + 8], mx[:, 0:8])])
            ckpt("p0")
            P.phase = "c1"
            c1 = coll_max(0, 0, 0)
            if os.environ.get("KPHASE") == "c1":
                dgrp("gpsimd", [(c_v, c_v.n), (c_s, c_s.n), (c_t, c_t.n)], pool_wg,
                     [(up_sp[0:128, 0:16], scl[:, 0:16]),
                      (up_sp[0:128, 16:24], mx[:, 0:8]),
                      (up_sp[0:1, 24:40], cms[0:1, 0:16]),
                      (up_sp[0:1, 40:41], cc_i[0].rearrange("a b -> b a")),
                      (up_sp[0:1, 48:56], cc_o[0].rearrange("a b -> b a"))])
            ckpt("c1")

            P.phase = "f1"
            for g in range(NG):
                wx = dgrp("sync", [(c_v, c_v.n), (c_s, c_s.n)], pool_ld,
                          [(ta[:, 0:D], x[g * 128:(g + 1) * 128, :])])
                vv = grp("vector", [wx, (c_s, c1)], c_v, lambda v, g=g: (
                    v.tensor_scalar(tb[:, 0:D], ta[:, 0:D], stat_m[:, g:g + 1], stat_r[:, g:g + 1],
                                    ALU.subtract, ALU.mult),
                    v.tensor_scalar(tb[:, 0:D], tb[:, 0:D], scl[:, 1:2], None, ALU.mult))[-1])
                sv = grp("scalar", [(c_v, vv)], c_s, lambda s: s.add(tb[:, 0:D], tb[:, 0:D], MAGIC))
                sv = grp("scalar", [], c_s, lambda s: s.add(tb[:, 0:D], tb[:, 0:D], -MAGIC))
                vv = grp("vector", [(c_s, sv)], c_v, lambda v: (
                    v.tensor_scalar(tb[:, 0:D], tb[:, 0:D], -QMAX - 1.0, QMAX, ALU.max, ALU.min),
                    v.tensor_scalar(tb[:, 0:D], tb[:, 0:D], scl[:, 0:1], None, ALU.mult),
                    v.tensor_copy(xnb[:], tb[:, 0:D]))[-1])
                spill_T(xnb, xq_dm, g, vv)

            def lpbq(w_ap, O, I, lhsT_sp):
                NO = O // 128
                IH = I if I <= 2048 else I // 2
                NB = IH // 32
                STL = [pool_st.all(), pool_st.all()]
                for ot in range(NO):
                    for ih in range(I // IH):
                        wld = dgrp("sync", [(c_v, c_v.n), (c_s, c_s.n)], pool_ld,
                                   [(ta[:, 0:IH], w_ap[ot * 128:(ot + 1) * 128, ih * IH:(ih + 1) * IH])])
                        vv = grp("vector", [wld], c_v, lambda v, IH=IH, NB=NB: v.reduce_max(
                            tcp[:, 0:NB].rearrange("p (a b) -> p a b", b=1),
                            ta[:, 0:IH].rearrange("p (a b) -> p a b", b=32),
                            axis=AX, apply_absolute_value=True))
                        vv = grp("vector", [], c_v, lambda v, NB=NB: v.tensor_scalar(
                            tcp[:, NB:2 * NB], tcp[:, 0:NB], 1.0 / 7.0, None, ALU.mult))
                        vv = grp("vector", [], c_v, lambda v, NB=NB: v.tensor_scalar_max(
                            tcp[:, 0:NB], tcp[:, NB:2 * NB], 1e-12))
                        vv = grp("vector", [], c_v, lambda v, NB=NB: v.reciprocal(
                            tb[:, 0:NB], tcp[:, 0:NB]))
                        vv = grp("vector", [], c_v, lambda v, IH=IH, NB=NB: v.tensor_tensor(
                            ta[:, 0:IH].rearrange("p (a b) -> p a b", b=32),
                            ta[:, 0:IH].rearrange("p (a b) -> p a b", b=32),
                            tb[:, 0:NB].rearrange("p (a b) -> p a b", b=1).broadcast_to([128, NB, 32]),
                            ALU.mult))
                        sv = grp("scalar", [(c_v, vv)], c_s,
                                 lambda s, IH=IH: s.add(ta[:, 0:IH], ta[:, 0:IH], MAGIC))
                        sv = grp("scalar", [], c_s,
                                 lambda s, IH=IH: s.add(ta[:, 0:IH], ta[:, 0:IH], -MAGIC))
                        vv = grp("vector", [(c_s, sv)], c_v, lambda v, IH=IH, NB=NB: (
                            v.tensor_scalar(ta[:, 0:IH], ta[:, 0:IH], -8.0, 7.0, ALU.max, ALU.min),
                            v.tensor_tensor(ta[:, 0:IH].rearrange("p (a b) -> p a b", b=32),
                                            ta[:, 0:IH].rearrange("p (a b) -> p a b", b=32),
                                            tcp[:, 0:NB].rearrange("p (a b) -> p a b", b=1).broadcast_to([128, NB, 32]),
                                            ALU.mult),
                            v.tensor_copy(ubank[:, 0:IH], ta[:, 0:IH]))[-1])
                        its = IH // 128
                        par = (ot * (I // IH) + ih) % 2
                        off = 2048 + par * 1536
                        sv2 = None
                        for it in range(its):
                            pi = it % 2
                            tv = grp("tensor", [(c_v, vv), (c_s, psb_ev[pi])], c_t,
                                     lambda t, it=it, pi=pi: t.transpose(psb[pi][:, 0:128],
                                                                         ubank[:, it * 128:(it + 1) * 128], ident[:]))
                            sv2 = grp("scalar", [(c_t, tv), (STL[par] if it == 0 else [])], c_s,
                                      lambda s, it=it, pi=pi, off=off: s.copy(
                                          ustage[:, off + it * 128:off + (it + 1) * 128], psb[pi][:, 0:128]))
                            psb_ev[pi] = sv2
                        STL[par] = dgrp("sync", [(c_s, sv2)], pool_st,
                                        [(lhsT_sp[ih * its * 128:(ih + 1) * its * 128,
                                                  ot * 128:(ot + 1) * 128].rearrange("(a p) b -> p a b", p=128),
                                          ustage[:, off:off + its * 128].rearrange("p (a b) -> p a b", b=128))])
                return pool_st.all()

            def mm_pass(lhsT_sp, NK, NO, src_dm, dst_sp, mx_col, wready):
                OH = 12
                stflush = pool_st.all()
                Tmm = [c_t.n, c_t.n]
                STM = [[], []]
                first = [True, True]
                for oh in range((NO + OH - 1) // OH):
                    o0, o1 = oh * OH, min((oh + 1) * OH, NO)
                    no = o1 - o0
                    wld = dgrp("gpsimd", [wready, (c_t, c_t.n)], pool_wg,
                               [(wbig[:, (k * no) * 128:((k + 1) * no) * 128],
                                 lhsT_sp[k * 128:(k + 1) * 128, o0 * 128:o1 * 128]) for k in range(NK)])
                    for tch in range(NT5):
                        j = tch % 2
                        sb = j * NK
                        mv = dgrp("sync", [(c_t, Tmm[j]), (stflush if first[j] else [])], pool_ld,
                                  [(ubank[:, (sb + k) * 512:(sb + k + 1) * 512],
                                    src_dm[k * 128:(k + 1) * 128, tch * 512:(tch + 1) * 512]) for k in range(NK)])
                        first[j] = False
                        for o in range(o0, o1):
                            pi = o % 2
                            tv = grp("tensor", [mv, wld, (c_s, ps_ev[pi])], c_t,
                                     lambda t, o=o, pi=pi, no=no, o0=o0, NK=NK, sb=sb: [
                                         t.matmul(ps[pi][:, 0:512],
                                                  wbig[:, (k * no + o - o0) * 128:(k * no + o - o0 + 1) * 128],
                                                  ubank[:, (sb + k) * 512:(sb + k + 1) * 512],
                                                  start=(k == 0), stop=(k == NK - 1))
                                         for k in range(NK)][-1])
                            sv = grp("scalar", [(c_t, tv), STM[pi]], c_s,
                                     lambda s, pi=pi, o=o: s.copy(fa[:] if o % 2 == 0 else fb[:],
                                                                  ps[pi][:, 0:512]))
                            ps_ev[pi] = sv
                            vv = grp("vector", [(c_s, sv)], c_v, lambda v, o=o: v.reduce_max(
                                tcp[:, 2:3], (fa[:] if o % 2 == 0 else fb[:]), axis=AX,
                                apply_absolute_value=True))
                            vv = grp("vector", [], c_v, lambda v, mx_col=mx_col: v.tensor_max(
                                mx[:, mx_col:mx_col + 1], mx[:, mx_col:mx_col + 1], tcp[:, 2:3]))
                            STM[pi] = dgrp("sync", [(c_v, vv)], pool_st,
                                           [(dst_sp[o * 128:(o + 1) * 128, tch * 512:(tch + 1) * 512],
                                             fa[:] if o % 2 == 0 else fb[:])])
                        Tmm[j] = c_t.n
                return pool_st.all()

            if os.environ.get("KPHASE") == "f1":
                dgrp("gpsimd", [(c_v, c_v.n), (c_s, c_s.n)], pool_wg,
                     [(up_sp[0:128, 0:16], scl[:, 0:16]),
                      (up_sp[0:128, 16:24], mx[:, 0:8])])
            ckpt("f1")
            P.phase = "ffnmm"
            lu = lpbq(dram["wup"], I3, D, wqTu)
            mm_pass(wqTu, NDM, NI3, xq_dm, up_sp, 1, lu)
            lg = lpbq(dram["wgate"], I3, D, wqTg)
            mm_pass(wqTg, NDM, NI3, xq_dm, gate_sp, 2, lg)

            ckpt("ffnmm")
            P.phase = "c2"
            c2 = coll_max(1, 1, 2)
            c3 = coll_max(2, 2, 4)

            P.phase = "f4"
            V4 = [c_v.n, c_v.n]
            S4 = [c_s.n, c_s.n]
            ST4 = [pool_st.all(), pool_st.all()]
            for idx in range(NI3 * NT5):
                it, tch = divmod(idx, NT5)
                j = idx % 2
                fin = fa if j == 0 else fa2
                fout = fb if j == 0 else fb2
                taj = ta[:, j * 512:(j + 1) * 512]
                tbj = tb[:, j * 512:(j + 1) * 512]
                rc = tcp[:, 2 + j:3 + j]
                wld = dgrp("sync", [(c_s, c3), (c_v, V4[j]), (c_s, S4[j]), ST4[j]], pool_ld,
                           [(fin[:], gate_sp[it * 128:(it + 1) * 128, tch * 512:(tch + 1) * 512])])
                vv = grp("vector", [wld], c_v,
                         lambda v, fin=fin, taj=taj: v.tensor_scalar(taj, fin[:], scl[:, 5:6], None, ALU.mult))
                sv = grp("scalar", [(c_v, vv)], c_s, lambda s, taj=taj: s.add(taj, taj, MAGIC))
                sv = grp("scalar", [], c_s, lambda s, taj=taj: s.add(taj, taj, -MAGIC))
                vv = grp("vector", [(c_s, sv)], c_v, lambda v, taj=taj: (
                    v.tensor_scalar(taj, taj, -QMAX - 1.0, QMAX, ALU.max, ALU.min),
                    v.tensor_scalar(taj, taj, scl[:, 4:5], None, ALU.mult))[-1])
                sv = grp("scalar", [(c_v, vv)], c_s,
                         lambda s, taj=taj, tbj=tbj: s.activation(out=tbj, in_=taj, func=AF.Sigmoid))
                sv = grp("scalar", [], c_s, lambda s, tbj=tbj: s.activation(
                    out=tbj, in_=tbj, func=AF.Copy, scale=65536.0, bias=MAGIC))
                sv = grp("scalar", [], c_s, lambda s, tbj=tbj: s.add(tbj, tbj, -MAGIC))
                S4[j] = sv
                vv = grp("vector", [(c_s, sv)], c_v, lambda v, taj=taj, tbj=tbj, rc=rc: (
                    v.tensor_scalar(tbj, tbj, 0.0, 65535.0, ALU.max, ALU.min),
                    v.tensor_scalar(tbj, tbj, 1.0 / 65536.0, None, ALU.mult),
                    v.tensor_tensor(taj, taj, tbj, ALU.mult),
                    v.reduce_max(rc, taj, axis=AX, apply_absolute_value=True))[-1])
                vv = grp("vector", [ST4[j]], c_v, lambda v, taj=taj, rc=rc, fout=fout: (
                    v.tensor_max(mx[:, 3:4], mx[:, 3:4], rc),
                    v.tensor_copy(fout[:], taj))[-1])
                V4[j] = vv
                ST4[j] = dgrp("sync", [(c_v, vv)], pool_st,
                              [(g2_sp[it * 128:(it + 1) * 128, tch * 512:(tch + 1) * 512], fout[:])])

            ckpt("f4")
            c4 = coll_max(3, 3, 6)

            P.phase = "f5"
            V5 = [c_v.n, c_v.n]
            S5 = [c_s.n, c_s.n]
            ST5 = [pool_st.all(), pool_st.all()]
            for idx in range(NI3 * NT5):
                it, tch = divmod(idx, NT5)
                j = idx % 2
                fin = fa if j == 0 else fa2
                fin2 = fb if j == 0 else fb2
                taj = ta[:, j * 512:(j + 1) * 512]
                tbj = tb[:, j * 512:(j + 1) * 512]
                rc = tcp[:, 2 + j:3 + j]
                wld = dgrp("sync", [(c_s, c4), (c_v, V5[j]), (c_s, S5[j]), ST5[j]], pool_ld,
                           [(fin[:], g2_sp[it * 128:(it + 1) * 128, tch * 512:(tch + 1) * 512]),
                            (fin2[:], up_sp[it * 128:(it + 1) * 128, tch * 512:(tch + 1) * 512])])
                vv = grp("vector", [wld], c_v, lambda v, fin=fin, fin2=fin2, taj=taj, tbj=tbj: (
                    v.tensor_scalar(taj, fin[:], scl[:, 7:8], None, ALU.mult),
                    v.tensor_scalar(tbj, fin2[:], scl[:, 3:4], None, ALU.mult))[-1])
                sv = grp("scalar", [(c_v, vv)], c_s, lambda s, taj=taj, tbj=tbj: (
                    s.add(taj, taj, MAGIC),
                    s.add(tbj, tbj, MAGIC))[-1])
                sv = grp("scalar", [], c_s, lambda s, taj=taj, tbj=tbj: (
                    s.add(taj, taj, -MAGIC),
                    s.add(tbj, tbj, -MAGIC))[-1])
                S5[j] = sv
                vv = grp("vector", [(c_s, sv)], c_v, lambda v, taj=taj, tbj=tbj, rc=rc: (
                    v.tensor_scalar(taj, taj, -QMAX - 1.0, QMAX, ALU.max, ALU.min),
                    v.tensor_scalar(tbj, tbj, -QMAX - 1.0, QMAX, ALU.max, ALU.min),
                    v.tensor_scalar(taj, taj, scl[:, 6:7], None, ALU.mult),
                    v.tensor_scalar(tbj, tbj, scl[:, 2:3], None, ALU.mult),
                    v.tensor_tensor(taj, taj, tbj, ALU.mult),
                    v.reduce_max(rc, taj, axis=AX, apply_absolute_value=True))[-1])
                vv = grp("vector", [], c_v, lambda v, taj=taj, rc=rc, fin=fin: (
                    v.tensor_max(mx[:, 4:5], mx[:, 4:5], rc),
                    v.tensor_copy(fin[:], taj))[-1])
                V5[j] = vv
                ST5[j] = dgrp("sync", [(c_v, vv)], pool_st,
                              [(o_sp[it * 128:(it + 1) * 128, tch * 512:(tch + 1) * 512], fin[:])])

            ckpt("f5")
            c5 = coll_max(4, 4, 8)

            P.phase = "f6"
            ld_ = lpbq(dram["wdown"], D, I3, wqTd)

            def ostage(k):
                return gat[:, k * 512:(k + 1) * 512] if k < 8 else ubank[:, (k - 8) * 512:(k - 7) * 512]

            OH = 4
            stf6 = pool_st.all()
            STF = [[], []]
            for oh in range(NDM // OH):
                o0 = oh * OH
                wld = dgrp("gpsimd", [ld_, (c_t, c_t.n)], pool_wg,
                           [(wbig[:, (k * OH) * 128:((k + 1) * OH) * 128],
                             wqTd[k * 128:(k + 1) * 128, o0 * 128:(o0 + OH) * 128]) for k in range(NI3)])
                for tch in range(NT5):
                    tq = c_t.n
                    V6 = [stf6 + [(c_v, c_v.n)], stf6 + [(c_v, c_v.n)]]
                    vq = 0
                    for k in range(NI3):
                        j = k % 2
                        fin = fa if j == 0 else fa2
                        taj = ta[:, j * 512:(j + 1) * 512]
                        wl = dgrp("sync", [(c_s, c5), (c_t, tq), V6[j]], pool_ld,
                                  [(fin[:], o_sp[k * 128:(k + 1) * 128, tch * 512:(tch + 1) * 512])])
                        vv = grp("vector", [wl], c_v,
                                 lambda v, fin=fin, taj=taj: v.tensor_scalar(taj, fin[:], scl[:, 9:10], None, ALU.mult))
                        sv = grp("scalar", [(c_v, vv)], c_s,
                                 lambda s, taj=taj: s.add(taj, taj, MAGIC))
                        sv = grp("scalar", [], c_s,
                                 lambda s, taj=taj: s.add(taj, taj, -MAGIC))
                        vq = grp("vector", [(c_s, sv)], c_v, lambda v, k=k, taj=taj: (
                            v.tensor_scalar(taj, taj, -QMAX - 1.0, QMAX, ALU.max, ALU.min),
                            v.tensor_scalar(taj, taj, scl[:, 8:9], None, ALU.mult),
                            v.tensor_copy(ostage(k), taj))[-1])
                        V6[j] = [(c_v, vq)]
                    for o in range(o0, o0 + OH):
                        pi = o % 2
                        fout6 = fo if o % 2 == 0 else xe[0]
                        tv = grp("tensor", [(c_v, vq), wld, (c_s, ps_ev[pi])], c_t,
                                 lambda t, o=o, pi=pi, o0=o0: [
                                     t.matmul(ps[pi][:, 0:512],
                                              wbig[:, (k * OH + o - o0) * 128:(k * OH + o - o0 + 1) * 128],
                                              ostage(k), start=(k == 0), stop=(k == NI3 - 1))
                                     for k in range(NI3)][-1])
                        sv = grp("scalar", [(c_t, tv), STF[pi]], c_s,
                                 lambda s, pi=pi, fout6=fout6: s.copy(fout6[:, 0:512], ps[pi][:, 0:512]))
                        ps_ev[pi] = sv
                        STF[pi] = dgrp("sync", [(c_s, sv)], pool_st,
                                       [(ffn_sp[o * 128:(o + 1) * 128, tch * 512:(tch + 1) * 512], fout6[:, 0:512])])

            def layer(l, src_dm, dst_dm):
                vz = grp("vector", [pool_ld.all(), pool_st.all()], c_v, lambda v: (
                    v.memset(hstate[:], 0.0),
                    v.memset(ccarry[:], 0.0))[-1])
                prevstore = []
                prevxdbl_t = 0
                for part in range(2):
                    wld = dgrp("gpsimd", [(c_t, c_t.n)], pool_wg,
                               [(wbig[:, (k * NDT) * 128:((k + 1) * NDT) * 128],
                                 dram[f"WinT{l}"][k * 128:(k + 1) * 128, part * DI:(part + 1) * DI])
                                for k in range(NDM)])
                    if part == 0:
                        wldx = dgrp("gpsimd", [(c_t, c_t.n)], pool_wg,
                                    [(wx_s[:, kt * 96:(kt + 1) * 96],
                                      dram[f"WxT{l}"][kt * 128:(kt + 1) * 128, :]) for kt in range(NDT)])
                    for tch in range(NT5):
                        mv = dgrp("sync", [(c_t, c_t.n), (pool_st.all() if tch == 0 and part == 0 else [])],
                                  pool_ld,
                                  [(ubank[:, k * 512:(k + 1) * 512],
                                    src_dm[k * 128:(k + 1) * 128, tch * 512:(tch + 1) * 512]) for k in range(NDM)])
                        uvv = vz
                        for o in range(NDT):
                            pi = o % 2
                            tv = grp("tensor", [mv, wld, (c_s, ps_ev[pi])], c_t,
                                     lambda t, o=o, pi=pi: [
                                         t.matmul(ps[pi][:, 0:512],
                                                  wbig[:, (k * NDT + o) * 128:(k * NDT + o + 1) * 128],
                                                  ubank[:, k * 512:(k + 1) * 512],
                                                  start=(k == 0), stop=(k == NDM - 1))
                                         for k in range(NDM)][-1])
                            if part == 0:
                                sv = grp("scalar", [(c_t, tv), (c_v, uvv),
                                                    prevstore, (c_t, prevxdbl_t)], c_s,
                                         lambda s, pi=pi: s.copy(uext[:, 3:515], ps[pi][:, 0:512]))
                                ps_ev[pi] = sv
                                cwc = l * NDT * DC + o * DC
                                uvv = grp("vector", [(c_s, sv)], c_v, lambda v, o=o, cwc=cwc: (
                                    v.tensor_copy(uext[:, 0:3], ccarry[:, o * 3:o * 3 + 3]),
                                    v.tensor_copy(ccarry[:, o * 3:o * 3 + 3], uext[:, 512:515]),
                                    v.tensor_scalar(ub[:], uext[:, 0:512], convw[:, cwc + 0:cwc + 1], None, ALU.mult),
                                    v.scalar_tensor_tensor(ub[:], uext[:, 1:513], convw[:, cwc + 1:cwc + 2], ub[:],
                                                           ALU.mult, ALU.add),
                                    v.scalar_tensor_tensor(ub[:], uext[:, 2:514], convw[:, cwc + 2:cwc + 3], ub[:],
                                                           ALU.mult, ALU.add),
                                    v.scalar_tensor_tensor(ub[:], uext[:, 3:515], convw[:, cwc + 3:cwc + 4], ub[:],
                                                           ALU.mult, ALU.add))[-1])
                                grp("scalar", [(c_v, uvv)], c_s,
                                    lambda s, o=o, l=l: s.activation(
                                        out=ustage[:, o * 512:(o + 1) * 512], in_=ub[:], func=AF.Silu,
                                        bias=convb[:, l * NDT + o:l * NDT + o + 1]))
                            else:
                                sv = grp("scalar", [(c_t, tv), prevstore], c_s,
                                         lambda s, pi=pi, o=o: s.activation(
                                             out=ustage[:, o * 512:(o + 1) * 512], in_=ps[pi][:, 0:512],
                                             func=AF.Silu))
                                ps_ev[pi] = sv
                        stv = c_s.n
                        dst_sp = us_sp if part == 0 else sz_sp
                        prevstore = dgrp("sync", [(c_s, stv)], pool_st,
                                         [(dst_sp.rearrange("(a p) t -> p a t", p=128)[:, :, tch * 512:(tch + 1) * 512],
                                           ustage[:].rearrange("p (a t) -> p a t", a=NDT))])
                        if part == 0:
                            pi = 2
                            tv = grp("tensor", [(c_s, stv), (c_s, ps_ev[pi]), wldx], c_t,
                                     lambda t, pi=pi: [
                                         t.matmul(ps[pi][0:96, 0:512],
                                                  wx_s[:, kt * 96:(kt + 1) * 96],
                                                  ustage[:, kt * 512:(kt + 1) * 512],
                                                  start=(kt == 0), stop=(kt == NDT - 1))
                                         for kt in range(NDT)][-1])
                            prevxdbl_t = tv
                            sv = grp("scalar", [(c_t, tv)], c_s,
                                     lambda s, tch=tch, pi=pi: s.copy(xdbl_b[0:96, tch * 512:(tch + 1) * 512],
                                                                      ps[pi][0:96, 0:512]))
                            ps_ev[pi] = sv
                wldo = dgrp("gpsimd", [(c_t, c_t.n)], pool_wg,
                            [(wbig[:, (k * NDM) * 128:((k + 1) * NDM) * 128],
                              dram[f"WoutT{l}"][k * 128:(k + 1) * 128, :]) for k in range(NDT)])
                wldd = dgrp("gpsimd", [(c_t, c_t.n)], pool_wg,
                            [(wdt_s[0:64, dt * 128:(dt + 1) * 128],
                              dram[f"WdtT{l}"][:, (DI // 2) * 0 + dt * 128:dt * 128 + 128]) for dt in range(NSC)])
                pa_done_st = pool_st.all()
                prev_scan_v = 0
                prev_gx = []
                da_rd = [0, 0]
                dtb_v = {}
                STW = [pool_st.all(), pool_st.all()]
                pend = None

                def emit_wout(gv, c):
                    cs = c * TC
                    for sub in range(TC // 512):
                        mvw = dgrp("sync", [(c_g, gv), (c_t, c_t.n)], pool_ld,
                                   [(ubank[:, k * 512:(k + 1) * 512],
                                     gx_o[c % 2][k * 128:(k + 1) * 128, sub * 512:(sub + 1) * 512])
                                    for k in range(NDT)])
                        for o in range(NDM):
                            pi = o % 2
                            fw = fo if o % 2 == 0 else xe[0]
                            tv = grp("tensor", [mvw, wldo, (c_s, ps_ev[pi])], c_t,
                                     lambda t, o=o, pi=pi: [
                                         t.matmul(ps[pi][:, 0:512],
                                                  wbig[:, (k * NDM + o) * 128:(k * NDM + o + 1) * 128],
                                                  ubank[:, k * 512:(k + 1) * 512],
                                                  start=(k == 0), stop=(k == NDT - 1))
                                         for k in range(NDT)][-1])
                            sv = grp("scalar", [(c_t, tv), STW[pi]], c_s,
                                     lambda s, pi=pi, fw=fw: s.copy(fw[:, 0:512], ps[pi][:, 0:512]))
                            ps_ev[pi] = sv
                            STW[pi] = dgrp("sync", [(c_s, sv)], pool_st,
                                           [(dst_dm[o * 128:(o + 1) * 128, cs + sub * 512:cs + (sub + 1) * 512],
                                             fw[:, 0:512])])

                for c in range(NTC):
                    cs = c * TC
                    for n in range(DS):
                        for (jsel, dstt) in ((n, bcB), (16 + n, bcC)):
                            pi = 2 + (n % 2)
                            tv = grp("tensor", [(c_s, c_s.n), (c_s, ps_ev[pi]), wsel], c_t,
                                     lambda t, jsel=jsel, pi=pi, cs=cs: t.matmul(
                                         ps[pi][:, 0:TC], selb[64:96, jsel * 128:(jsel + 1) * 128],
                                         xdbl_b[64:96, cs:cs + TC], start=True, stop=True))
                            sv = grp("scalar", [(c_t, tv), (c_v, prev_scan_v)], c_s,
                                     lambda s, n=n, dstt=dstt, pi=pi: s.copy(dstt[:, n * TC:(n + 1) * TC],
                                                                             ps[pi][:, 0:TC]))
                            ps_ev[pi] = sv
                    bc_s = c_s.n
                    for dt in range(NSC):
                        mvu = dgrp("sync", [(c_v, prev_scan_v), (pa_done_st if c == 0 else None)], pool_ld,
                                   [(uball[:, dt * TC:(dt + 1) * TC], us_sp[dt * 128:(dt + 1) * 128, cs:cs + TC]),
                                    (zball[:, dt * TC:(dt + 1) * TC], sz_sp[dt * 128:(dt + 1) * 128, cs:cs + TC])])
                        pi = 2
                        tv = grp("tensor", [(c_s, ps_ev[pi]), wldd], c_t,
                                 lambda t, dt=dt, pi=pi, cs=cs: t.matmul(
                                     ps[pi][:, 0:TC], wdt_s[0:64, dt * 128:(dt + 1) * 128],
                                     xdbl_b[0:64, cs:cs + TC], start=True, stop=True))
                        sv = grp("scalar", [(c_t, tv), (c_v, prev_scan_v),
                                            (c_v, dtb_v.get(dt - 1, 0))], c_s,
                                 lambda s, dt=dt, pi=pi, l=l: s.activation(
                                     out=ta[:, 0:TC], in_=ps[pi][:, 0:TC],
                                     func=AF.Identity, bias=bdt[:, l * NSC + dt:l * NSC + dt + 1]))
                        ps_ev[pi] = sv
                        sv = grp("scalar", [(c_s, sv)], c_s,
                                 lambda s: s.activation(out=tb[:, 0:TC], in_=ta[:, 0:TC], func=AF.Abs))
                        sv = grp("scalar", [], c_s,
                                 lambda s: s.activation(out=tb[:, 0:TC], in_=tb[:, 0:TC], func=AF.Exp, scale=-1.0))
                        sv = grp("scalar", [], c_s,
                                 lambda s: s.activation(out=tb[:, 0:TC], in_=tb[:, 0:TC], func=AF.Ln, bias=1.0))
                        dtb_v[dt] = grp("vector", [(c_s, sv), mvu], c_v, lambda v, dt=dt: (
                            v.tensor_scalar_max(ta[:, 0:TC], ta[:, 0:TC], 0.0),
                            v.tensor_tensor(dtball[:, dt * TC:(dt + 1) * TC], ta[:, 0:TC], tb[:, 0:TC], ALU.add),
                            v.tensor_tensor(dtuall[:, dt * TC:(dt + 1) * TC],
                                            dtball[:, dt * TC:(dt + 1) * TC],
                                            uball[:, dt * TC:(dt + 1) * TC], ALU.mult))[-1])
                    for dt in range(NSC):
                        for n in range(DS):
                            pp = (dt * DS + n) % 2
                            sv = grp("scalar", [(c_s, bc_s), (c_v, da_rd[pp]), (c_v, dtb_v[dt])], c_s,
                                     lambda s, dt=dt, n=n, pp=pp, l=l: s.activation(
                                         out=da[pp][:], in_=dtball[:, dt * TC:(dt + 1) * TC], func=AF.Exp,
                                         scale=Aneg[:, l * NSC * DS + dt * DS + n:l * NSC * DS + dt * DS + n + 1]))
                            hs = dt * DS + n
                            if os.environ.get("KDBG") and l == 0 and c == 0 and dt == 0 and n == 0:
                                dbg_v = grp("vector", [(c_s, sv)], c_v, lambda v: (
                                    v.tensor_tensor(dbu[:], dtuall[:, 0:TC], bcB[:, 0:TC], ALU.mult),
                                    v.tensor_copy(ta[:, 0:TC], da[pp][:]),
                                    v.tensor_copy(tb[:, 0:TC], dbu[:]),
                                    v.tensor_copy(tcp[:, 0:TC // 2], dtball[:, 0:TC // 2]))[-1])
                                st1 = dgrp("sync", [(c_v, dbg_v)], pool_st, [
                                    (out[0:128, 0:TC], ta[:, 0:TC]),
                                    (out[128:256, 0:TC], tb[:, 0:TC]),
                                    (out[256:384, 0:TC // 2], tcp[:, 0:TC // 2])])
                                dbg_v2 = grp("vector", [(c_v, dbg_v), st1], c_v, lambda v: (
                                    v.tensor_copy(ta[:, 0:TC], bcB[:, 0:TC]),
                                    v.tensor_copy(tb[:, 0:TC], uball[:, 0:TC]),
                                    v.tensor_copy(tcp[0:96, 0:512], xdbl_b[0:96, 0:512]))[-1])
                                dgrp("sync", [(c_v, dbg_v2)], pool_st, [
                                    (out[384:512, 0:TC], ta[:, 0:TC]),
                                    (out[256:384, TC // 2:TC // 2 + TC] if False else out[128 + 256:128 + 384, 0:0 + TC], tb[:, 0:TC])])
                                raise StopIteration
                            da_rd[pp] = grp("vector", [(c_s, sv)], c_v, lambda v, dt=dt, n=n, pp=pp, hs=hs: (
                                v.tensor_tensor(dbu[:], dtuall[:, dt * TC:(dt + 1) * TC],
                                                bcB[:, n * TC:(n + 1) * TC], ALU.mult),
                                v.tensor_tensor_scan(hb[:], da[pp][:], dbu[:],
                                                     hstate[:, hs:hs + 1], ALU.mult, ALU.add),
                                v.tensor_copy(hstate[:, hs:hs + 1], hb[:, TC - 1:TC]),
                                v.tensor_tensor(hcb[:], hb[:], bcC[:, n * TC:(n + 1) * TC], ALU.mult),
                                (v.tensor_copy(yball[:, dt * TC:(dt + 1) * TC], hcb[:]) if n == 0 else
                                 v.tensor_tensor(yball[:, dt * TC:(dt + 1) * TC],
                                                 yball[:, dt * TC:(dt + 1) * TC], hcb[:], ALU.add)))[-1])
                        grp("vector", [prev_gx], c_v, lambda v, dt=dt, l=l: (
                            v.scalar_tensor_tensor(yball[:, dt * TC:(dt + 1) * TC],
                                                   uball[:, dt * TC:(dt + 1) * TC],
                                                   Dvt[:, l * NSC + dt:l * NSC + dt + 1],
                                                   yball[:, dt * TC:(dt + 1) * TC], ALU.mult, ALU.add),
                            v.tensor_tensor(gat[:, dt * TC:(dt + 1) * TC], yball[:, dt * TC:(dt + 1) * TC],
                                            zball[:, dt * TC:(dt + 1) * TC], ALU.mult))[-1])
                        if dt == 2 and pend is not None:
                            emit_wout(*pend)
                            pend = None
                    prev_scan_v = c_v.n
                    prev_gx = dgrp("sync", [(c_v, prev_scan_v), (c_g, c_g.n)], pool_st,
                                   [(gx_i[c % 2][:].rearrange("(a p) t -> p a t", p=128),
                                     gat[:].rearrange("p (a t) -> p a t", a=NSC))])
                    gv = grp("gpsimd", [prev_gx], c_g, lambda g, c=c: g.collective_compute(
                        "AllGather", ALU.bypass, ins=[gx_i[c % 2][:]], outs=[gx_o[c % 2][:]],
                        replica_groups=PAIRS))
                    pend = (gv, c)
                if pend is not None:
                    emit_wout(*pend)
                    pend = None

            P.phase = "layers"
            ckpt("f6")
            try:
                layer(0, xn_dm, hmid_sp)
                layer(1, hmid_sp, h2_sp)
            except StopIteration:
                pass

            P.phase = "asm"
            SKIP_ASM = bool(os.environ.get("KDBG"))
            outst = []
            for tch in range([] if SKIP_ASM else range(NT5)) if False else (range(0) if SKIP_ASM else range(NT5)):
                mv = dgrp("sync", [(c_v, c_v.n), (pool_st.all() if tch == 0 else [])], pool_ld,
                          [(uball[:, k * 512:(k + 1) * 512],
                            h2_sp[k * 128:(k + 1) * 128, tch * 512:(tch + 1) * 512]) for k in range(NDM)] +
                          [(zball[:, k * 512:(k + 1) * 512],
                            ffn_sp[k * 128:(k + 1) * 128, tch * 512:(tch + 1) * 512]) for k in range(NDM)])
                vv = grp("vector", [mv], c_v, lambda v: [
                    v.tensor_tensor(yball[:, k * 512:(k + 1) * 512], uball[:, k * 512:(k + 1) * 512],
                                    zball[:, k * 512:(k + 1) * 512], ALU.add) for k in range(NDM)][-1])
                for g4 in range(4):
                    g = tch * 4 + g4
                    for o in range(NDM):
                        pi = 2 + (o % 2)
                        tv = grp("tensor", [(c_v, vv), (c_s, ps_ev[pi])], c_t,
                                 lambda t, o=o, g4=g4, pi=pi: t.transpose(
                                     ps[pi][:, 0:128], yball[:, o * 512 + g4 * 128:o * 512 + (g4 + 1) * 128],
                                     identf[:]))
                        sv = grp("scalar", [(c_t, tv), outst], c_s,
                                 lambda s, o=o, pi=pi: s.copy(tb[:, o * 128:(o + 1) * 128], ps[pi][:, 0:128]))
                        ps_ev[pi] = sv
                    wx = dgrp("sync", [(c_v, c_v.n)], pool_ld, [(ta[:, 0:D], x[g * 128:(g + 1) * 128, :])])
                    vv2 = grp("vector", [wx, (c_s, c_s.n)], c_v, lambda v, g=g: (
                        v.tensor_scalar(tcp[:, 0:D], ta[:, 0:D],
                                        stat_m[:, g:g + 1], stat_r[:, g:g + 1], ALU.subtract, ALU.mult),
                        v.tensor_tensor(tb[:, 0:D], tb[:, 0:D], tcp[:, 0:D], ALU.add),
                        v.tensor_tensor(tb[:, 0:D], tb[:, 0:D], ta[:, 0:D], ALU.add))[-1])
                    outst = dgrp("sync", [(c_v, vv2)], pool_st, [(out[g * 128:(g + 1) * 128, :], tb[:, 0:D])])

        try:
            emit_all()
        except StopIteration:
            pass

        fin_deps = pool_ld.all() + pool_st.all() + pool_wg.all()

        def f_fin(e, fin_deps=fin_deps):
            for (c, v) in fin_deps:
                e.wait_ge(c.sem, v)
        P.on("gpsimd", f_fin)

        with nc.Block() as blk:
            @blk.sync
            def _(sy):
                P.emit("sync", sy, nc)

            @blk.vector
            def _(v):
                P.emit("vector", v, nc)

            @blk.scalar
            def _(s):
                P.emit("scalar", s, nc)

            @blk.gpsimd
            def _(g):
                P.emit("gpsimd", g, nc)

            @blk.tensor
            def _(t):
                P.emit("tensor", t, nc)

    return dram


def make_in_maps(inputs, T=4096):
    xf = np.ascontiguousarray(inputs["L0_attn_x"])
    ident = np.eye(128, dtype=np.float32)
    kconst = np.tile(np.array([1e-5, -0.5, MAGIC, -MAGIC, 65536.0, 1.0, 0, 0], np.float32), (128, 1))
    selbc = np.zeros((128, 32 * 128), np.float32)
    for jj in range(32):
        selbc[64 + jj, jj * 128:(jj + 1) * 128] = 1.0
    in_maps = []
    for c in range(N_CORES):
        b, j = c // 2, c % 2
        perm = np.arange(DI) if j == 0 else np.r_[DI // 2:DI, 0:DI // 2]
        im = {"x": np.ascontiguousarray(xf[b][:T]), "ident": ident, "kconst": kconst, "selbc": selbc}
        for l in range(2):
            Wi = inputs["L0_attn_W_in"][l]
            Wu, Wz = Wi[:DI][perm], Wi[DI:][perm]
            im[f"WinT{l}"] = np.ascontiguousarray(np.concatenate([Wu, Wz], 0).T)
            im[f"WxT{l}"] = np.ascontiguousarray(inputs["L0_attn_W_x"][l][:, perm].T)
            im[f"WdtT{l}"] = np.ascontiguousarray(inputs["L0_attn_W_dt"][l][perm].T)
            im[f"WoutT{l}"] = np.ascontiguousarray(inputs["L0_attn_W_out"][l].T)
            im[f"convw{l}"] = np.ascontiguousarray(inputs["L0_attn_conv_w"][l][perm])
            im[f"convb{l}"] = np.ascontiguousarray(inputs["L0_attn_conv_b"][l][perm][:, None])
            im[f"bdt{l}"] = np.ascontiguousarray(inputs["L0_attn_b_dt"][l][perm][:, None])
            im[f"Alog{l}"] = np.ascontiguousarray(inputs["L0_attn_A_log"][l][perm])
            im[f"Dv{l}"] = np.ascontiguousarray(inputs["L0_attn_D"][l][perm][:, None])
        im["wup"] = np.ascontiguousarray(inputs["L0_ffn_w_up"])
        im["wgate"] = np.ascontiguousarray(inputs["L0_ffn_w_gate"])
        im["wdown"] = np.ascontiguousarray(inputs["L0_ffn_w_down"])
        in_maps.append(im)
    return in_maps


def kernel(**inputs):
    T = 4096
    nc = bass.Bass("TRN2", target_bir_lowering=False, detect_race_conditions=False)
    build(nc, T)
    in_maps = make_in_maps(inputs, T)
    res = run_bass_kernel_spmd(nc, in_maps, core_ids=list(range(N_CORES)))
    out = np.stack([res.results[2 * b]["out"] for b in range(4)], 0)
    return out.astype(np.float32)


# version 09 (final)
# speedup vs baseline: 1.2339x; 1.0053x over previous
import os
import numpy as np
import ml_dtypes
import concourse.bass as bass
import concourse.mybir as mybir
from concourse.bass_utils import run_bass_kernel_spmd

F32 = mybir.dt.float32
BF16 = mybir.dt.bfloat16
F16 = mybir.dt.float16
AF = mybir.ActivationFunctionType
ALU = mybir.AluOpType
AX = mybir.AxisListType.X

D, DI, DS, DTR, DC, I3 = 1024, 2048, 16, 64, 4, 3072
NDT = DI // 128
NSC = NDT // 2
NDM = D // 128
NI3 = I3 // 128
MAGIC = float(1.5 * 2 ** 23)
QMAX = 32767.0
N_CORES = 8
PAIRS = [[0, 1], [2, 3], [4, 5], [6, 7]]
ALL8 = [list(range(N_CORES))]


class Ctr:
    def __init__(self, sem, step=1):
        self.sem, self.n, self.step = sem, 0, step


class Prog:
    def __init__(self):
        self.items = []
        self.phase = "init"

    def on(self, eng, fn):
        self.items.append((eng, fn, self.phase))

    def emit(self, name, eng, nc=None):
        for e, fn, ph in self.items:
            if e == name:
                if nc is not None:
                    with nc.named_scope(ph):
                        fn(eng)
                else:
                    fn(eng)


def build(nc, T):
    TC = min(512, T)
    NTC = T // TC
    NG = T // 128
    NT5 = T // 512
    P = Prog()
    dram = {}

    def din(name, shape, dt):
        dram[name] = nc.dram_tensor(name, shape, dt, kind="ExternalInput").ap()
        return dram[name]

    DUMP = bool(os.environ.get("KDUMP"))

    def dint(name, shape, dt):
        kind = "ExternalOutput" if (DUMP and name in (
            "xn_dm", "xq_dm", "us_sp", "sz_sp", "hmid_sp", "h2_sp",
            "up_sp", "gate_sp", "g2_sp", "o_sp", "ffn_sp", "wqTu", "wqTg", "wqTd")) else "Internal"
        dram[name] = nc.dram_tensor(name, shape, dt, kind=kind).ap()
        return dram[name]

    x = din("x", [T, D], F32)
    ident_in = din("ident", [128, 128], F32)
    kconst_in = din("kconst", [128, 8], F32)
    din("selbc", [128, 32 * 128], F32)
    for l in range(2):
        din(f"WinT{l}", [D, 2 * DI], F32)
        din(f"WxT{l}", [DI, 96], F32)
        din(f"WdtT{l}", [DTR, DI], F32)
        din(f"WoutT{l}", [DI, D], F32)
        din(f"convw{l}", [DI, DC], F32)
        din(f"convb{l}", [DI, 1], F32)
        din(f"bdt{l}", [DI, 1], F32)
        din(f"Alog{l}", [DI, DS], F32)
        din(f"Dv{l}", [DI, 1], F32)
    din("wup", [I3, D], F32)
    din("wgate", [I3, D], F32)
    din("wdown", [D, I3], F32)
    out = nc.dram_tensor("out", [T, D], F32, kind="ExternalOutput").ap()

    xn_dm = dint("xn_dm", [D, T], BF16)
    xq_dm = dint("xq_dm", [D, T], BF16)
    us_sp = dint("us_sp", [DI, T], BF16)
    sz_sp = dint("sz_sp", [DI, T], BF16)
    hmid_sp = dint("hmid_sp", [D, T], BF16)
    h2_sp = dint("h2_sp", [D, T], BF16)
    up_sp = dint("up_sp", [I3, T], F16)
    gate_sp = dint("gate_sp", [I3, T], F16)
    g2_sp = dint("g2_sp", [I3, T], F16)
    o_sp = dint("o_sp", [I3, T], F16)
    ffn_sp = dint("ffn_sp", [D, T], BF16)
    wqTu = dint("wqTu", [D, I3], BF16)
    wqTg = dint("wqTg", [D, I3], BF16)
    wqTd = dint("wqTd", [I3, D], BF16)
    cc_i = [dint(f"cc_i{i}", [1, 1], F32) for i in range(5)]
    cc_o = [dint(f"cc_o{i}", [N_CORES, 1], F32) for i in range(5)]
    gx_i = [dint(f"gx_i{i}", [DI // 2, TC], BF16) for i in range(2)]
    gx_o = [dint(f"gx_o{i}", [DI, TC], BF16) for i in range(2)]

    from contextlib import ExitStack
    with ExitStack() as st_:
        def sbt(name, shape, dt):
            return st_.enter_context(nc.sbuf_tensor(name, shape, dt))

        ident = sbt("identb_s", [128, 128], BF16)
        identf = sbt("identf_s", [128, 128], F32)
        ones1 = sbt("ones1", [1, 128], F32)
        onesb = sbt("onesb", [1, 128], BF16)
        stat_m = sbt("stat_m", [128, NG], F32)
        stat_r = sbt("stat_r", [128, NG], F32)
        Aneg = sbt("Aneg", [128, 2 * NSC * DS], F32)
        convw = sbt("convw", [128, 2 * NDT * DC], F32)
        convb = sbt("convb", [128, 2 * NDT], F32)
        bdt = sbt("bdt", [128, 2 * NSC], F32)
        Dvt = sbt("Dvt", [128, 2 * NSC], F32)
        hstate = sbt("hstate", [128, NSC * DS], F32)
        scl = sbt("scl", [128, 16], F32)
        mx = sbt("mx", [128, 8], F32)
        statv = sbt("statv", [128, NG], F32)
        cms = sbt("cms", [128, 16], F32)
        kconst = sbt("kconst_s", [128, 8], F32)
        for ci, cvv in enumerate([1e-5, -0.5, MAGIC, -MAGIC, 65536.0, 1.0]):
            nc.const_aps.register(F32, cvv, kconst[:, ci:ci + 1])

        wbig = sbt("wbig", [128, 16384], BF16)
        selb = sbt("selb", [128, 32 * 128], BF16)
        wx_s = sbt("wx_s", [128, NDT * 96], BF16)
        wdt_s = sbt("wdt_s", [128, NSC * 128], BF16)
        bcB = sbt("bcB", [128, DS * TC], BF16)
        bcC = sbt("bcC", [128, DS * TC], BF16)
        uball = sbt("uball", [128, NSC * TC], BF16)
        zball = sbt("zball", [128, NSC * TC], BF16)
        dtball = sbt("dtball", [128, NSC * TC], BF16)
        dtuall = sbt("dtuall", [128, NSC * TC], BF16)
        yball = sbt("yball", [128, NSC * TC], F32)
        ustage = sbt("ustage", [128, NDT * 512], BF16)
        gat = sbt("gat", [128, NSC * TC], BF16)
        ubank = sbt("ubank", [128, NDT * 512], BF16)
        xdbl_b = sbt("xdbl_b", [128, T], BF16)
        ta = sbt("ta", [128, 1536], F32)
        tb = sbt("tb", [128, 1024], F32)
        tcp = sbt("tcp", [128, 1024], F32)
        xnb = sbt("xnb", [128, 1024], BF16)
        xe = [sbt(f"xe{i}", [128, 512], BF16) for i in range(2)]
        uext = sbt("uext", [128, 515], BF16)
        ub = sbt("ub", [128, 512], BF16)
        ccarry = sbt("ccarry", [128, NDT * 3], BF16)
        da = [sbt(f"da{i}", [128, TC], BF16) for i in range(2)]
        dbu = sbt("dbu", [128, TC], BF16)
        hb = sbt("hb", [128, TC], BF16)
        hcb = sbt("hcb", [128, TC], BF16)
        fa = sbt("fa", [128, 512], F16)
        fb = sbt("fb", [128, 512], F16)
        fa2 = sbt("fa2", [128, 512], F16)
        fb2 = sbt("fb2", [128, 512], F16)
        fo = sbt("fo", [128, 512], BF16)

        ps = [st_.enter_context(nc.psum_tensor(f"ps{i}", [128, 512], F32)) for i in range(4)]
        ps_ev = [0, 0, 0, 0]
        psb = [st_.enter_context(nc.psum_tensor(f"psb{i}", [128, 128], BF16)) for i in range(2)]
        psb_ev = [0, 0]

        sems = {}
        for nm, stp in [("v", 1), ("s", 1), ("t", 1), ("g", 1)]:
            sems[nm] = Ctr(st_.enter_context(nc.semaphore("sem_" + nm)), stp)
        c_v, c_s, c_t, c_g = (sems[k] for k in ["v", "s", "t", "g"])
        self_ctr = {"vector": c_v, "scalar": c_s, "tensor": c_t}

        class Pool:
            def __init__(self, tag, K):
                self.ctrs = [Ctr(st_.enter_context(nc.semaphore(f"dma_{tag}{i}")), 16) for i in range(K)]
                self.i = 0

            def take(self):
                c = self.ctrs[self.i % len(self.ctrs)]
                self.i += 1
                return c

            def all(self):
                return [(c, c.n) for c in self.ctrs if c.n > 0]

        pool_ld = Pool("ld", 6)
        pool_st = Pool("st", 6)
        pool_wg = Pool("wg", 4)

        def flat(waits):
            o = []
            for w in waits:
                if isinstance(w, list):
                    o.extend(w)
                elif w is not None:
                    o.append(w)
            return o

        def grp(eng_name, waits, ctr, body):
            waits = flat(waits)
            sc = self_ctr.get(eng_name)
            if sc is not None and sc.n > 0:
                waits = waits + [(sc, sc.n)]
            if eng_name == "gpsimd":
                if c_g.n > 0:
                    waits = waits + [(c_g, c_g.n)]
            waits = [(c, v) for (c, v) in waits if v > 0]
            dedup = {}
            for (c, v) in waits:
                dedup[id(c)] = (c, max(v, dedup.get(id(c), (c, 0))[1]))
            waits = list(dedup.values())

            def f(e, waits=waits, body=body, ctr=ctr):
                for (c, v) in waits:
                    e.wait_ge(c.sem, v)
                last = body(e)
                if ctr is not None:
                    last.then_inc(ctr.sem, ctr.step)
            P.on(eng_name, f)
            if ctr is None:
                return None
            ctr.n += ctr.step
            return ctr.n

        def dgrp(eng_name, waits, pool, dmas, slow=False):
            waits = [(c, v) for (c, v) in flat(waits) if v > 0]
            dedup = {}
            for (c, v) in waits:
                dedup[id(c)] = (c, max(v, dedup.get(id(c), (c, 0))[1]))
            waits = list(dedup.values())
            plan = []
            deps = {}
            for d in dmas:
                c = pool.take()
                plan.append((c, c.n, d))
                c.n += 16
                deps[id(c)] = (c, c.n)

            def f(e, waits=waits, plan=plan, slow=slow):
                for (c, v) in waits:
                    e.wait_ge(c.sem, v)
                for (c, prev, (o_, i_)) in plan:
                    if prev > 0:
                        e.wait_ge(c.sem, prev)
                    e.dma_start(out=o_, in_=i_, allow_slow_non_contiguous=slow).then_inc(c.sem, 16)
            P.on(eng_name, f)
            return list(deps.values())

        TR = {"st": []}

        wi = dgrp("sync", [], pool_ld, [(identf[:], ident_in[:]), (kconst[:], kconst_in[:])])
        wA = dgrp("sync", [], pool_ld, [
            (ta[:, l * NSC * DS:(l + 1) * NSC * DS].rearrange("p (a b) -> p a b", a=NSC),
             dram[f"Alog{l}"][0:DI // 2, :].rearrange("(a p) b -> p a b", p=128))
            for l in range(2)])
        wP = dgrp("sync", [], pool_ld, sum([[
            (convw[:, l * NDT * DC:(l + 1) * NDT * DC].rearrange("p (a b) -> p a b", a=NDT),
             dram[f"convw{l}"].rearrange("(a p) b -> p a b", p=128)),
            (convb[:, l * NDT:(l + 1) * NDT].rearrange("p (a b) -> p a b", b=1),
             dram[f"convb{l}"].rearrange("(a p) b -> p a b", p=128)),
            (bdt[:, l * NSC:(l + 1) * NSC].rearrange("p (a b) -> p a b", b=1),
             dram[f"bdt{l}"][0:DI // 2].rearrange("(a p) b -> p a b", p=128)),
            (Dvt[:, l * NSC:(l + 1) * NSC].rearrange("p (a b) -> p a b", b=1),
             dram[f"Dv{l}"][0:DI // 2].rearrange("(a p) b -> p a b", p=128)),
        ] for l in range(2)], []), slow=True)
        wsel = dgrp("gpsimd", [], pool_wg, [(selb[:], dram["selbc"][:])])
        sv = grp("scalar", [wA], c_s, lambda s: s.activation(
            out=Aneg[:, 0:2 * NSC * DS], in_=ta[:, 0:2 * NSC * DS], func=AF.Exp))
        init_v = grp("vector", [wi, wP, (c_s, sv)], c_v, lambda v: (
            v.tensor_copy(ident[:], identf[:]),
            v.memset(ones1[:], 1.0),
            v.memset(onesb[:], 1.0),
            v.memset(mx[:], 0.0),
            v.tensor_scalar(Aneg[:, 0:2 * NSC * DS], Aneg[:, 0:2 * NSC * DS], -1.0, None, ALU.mult))[-1])

        SPT = {0: [], 1: []}
        def spill_T(src_bf, dst_dm, g, vready):
            par = g % 2
            off = par * 1024
            sv_ = None
            for o in range(NDM):
                pi = o % 2
                tv = grp("tensor", [(c_v, vready), (c_s, psb_ev[pi])], c_t,
                         lambda t, o=o, pi=pi, src_bf=src_bf: t.transpose(
                             psb[pi][:, 0:128], src_bf[:, o * 128:(o + 1) * 128], ident[:]))
                sv_ = grp("scalar", [(c_t, tv), (SPT[par] if o == 0 else [])], c_s,
                          lambda s, o=o, pi=pi, off=off: s.copy(
                              ustage[:, off + o * 128:off + (o + 1) * 128], psb[pi][:, 0:128]))
                psb_ev[pi] = sv_
            SPT[par] = dgrp("sync", [(c_s, sv_)], pool_st,
                            [(dst_dm[0:D, g * 128:(g + 1) * 128].rearrange("(a p) b -> p a b", p=128),
                              ustage[:, off:off + 1024].rearrange("p (a b) -> p a b", b=128))])

        def ckpt(name):
            if os.environ.get("KPHASE") == name:
                raise StopIteration

        def emit_all():
            P.phase = "p0"
            for g in range(NG):
                wx = dgrp("sync", [(c_v, init_v if g == 0 else c_v.n), (c_s, c_s.n)], pool_ld,
                          [(ta[:, 0:D], x[g * 128:(g + 1) * 128, :])])
                sv = grp("scalar", [wx], c_s,
                         lambda s: s.activation(out=tb[:, 0:D], in_=ta[:, 0:D], func=AF.Square))
                vv = grp("vector", [(c_s, sv)], c_v, lambda v, g=g: (
                    v.reduce_sum(stat_m[:, g:g + 1], ta[:, 0:D], axis=AX),
                    v.reduce_sum(stat_r[:, g:g + 1], tb[:, 0:D], axis=AX))[-1])
            vv = grp("vector", [], c_v, lambda v: (
                v.tensor_scalar(stat_m[:, 0:NG], stat_m[:, 0:NG], 1.0 / D, None, ALU.mult),
                v.tensor_scalar(stat_r[:, 0:NG], stat_r[:, 0:NG], 1.0 / D, None, ALU.mult),
                v.tensor_tensor(tcp[:, 0:NG], stat_m[:, 0:NG], stat_m[:, 0:NG], ALU.mult),
                v.tensor_sub(stat_r[:, 0:NG], stat_r[:, 0:NG], tcp[:, 0:NG]))[-1])
            sv = grp("scalar", [(c_v, vv)], c_s, lambda s: s.activation(
                out=statv[:, 0:NG], in_=stat_r[:, 0:NG], func=AF.Ln, bias=1e-5))
            sv = grp("scalar", [], c_s, lambda s: s.activation(
                out=stat_r[:, 0:NG], in_=statv[:, 0:NG], func=AF.Exp, scale=-0.5))
            stats_done_s = sv
            for g in range(NG):
                wx = dgrp("sync", [(c_v, c_v.n), (c_s, c_s.n)], pool_ld,
                          [(ta[:, 0:D], x[g * 128:(g + 1) * 128, :])])
                vv = grp("vector", [wx, (c_s, stats_done_s)], c_v, lambda v, g=g: (
                    v.tensor_scalar(tb[:, 0:D], ta[:, 0:D], stat_m[:, g:g + 1], stat_r[:, g:g + 1],
                                    ALU.subtract, ALU.mult),
                    v.reduce_max(tcp[:, 0:1], tb[:, 0:D], axis=AX, apply_absolute_value=True))[-1])
                vv = grp("vector", [], c_v, lambda v, g=g: (
                    v.tensor_max(mx[:, 0:1], mx[:, 0:1], tcp[:, 0:1]),
                    v.tensor_copy(xnb[:], tb[:, 0:D]))[-1])
                spill_T(xnb, xn_dm, g, vv)

            def coll_max(idx, mx_col, scl_col):
                pi = 3
                tv = grp("tensor", [(c_v, c_v.n), (c_s, ps_ev[pi])], c_t,
                         lambda t, mx_col=mx_col: t.transpose(ps[pi][0:1, 0:128], mx[:, mx_col:mx_col + 1], identf[:]))
                sv = grp("scalar", [(c_t, tv)], c_s, lambda s: s.copy(ta[0:1, 0:128], ps[pi][0:1, 0:128]))
                ps_ev[pi] = sv
                vv = grp("vector", [(c_s, sv)], c_v,
                         lambda v: v.reduce_max(cms[0:1, 4:5], ta[0:1, 0:128], axis=AX))
                wd = dgrp("sync", [(c_v, vv)], pool_st, [(cc_i[idx][:], cms[0:1, 4:5])])
                gv = grp("gpsimd", [wd], c_g, lambda g, idx=idx: g.collective_compute(
                    "AllGather", ALU.bypass, ins=[cc_i[idx][:]], outs=[cc_o[idx][:]],
                    replica_groups=ALL8))
                wr = dgrp("sync", [(c_g, gv)], pool_ld, [(cms[0:1, 8:8 + N_CORES], cc_o[idx].rearrange("a b -> b a"))])
                vv = grp("vector", [wr], c_v, lambda v: v.reduce_max(
                    cms[0:1, 4:5], cms[0:1, 8:8 + N_CORES], axis=AX))
                vv = grp("vector", [], c_v, lambda v: v.tensor_scalar(
                    cms[0:1, 7:8], cms[0:1, 4:5], 1.0 / QMAX, None, ALU.mult))
                vv = grp("vector", [], c_v, lambda v: v.tensor_scalar_max(
                    cms[0:1, 5:6], cms[0:1, 7:8], 1e-12))
                vv = grp("vector", [], c_v, lambda v: v.reciprocal(
                    cms[0:1, 6:7], cms[0:1, 5:6]))
                tv = grp("tensor", [(c_v, vv), (c_s, ps_ev[pi])], c_t, lambda t: t.matmul(
                    ps[pi][:, 0:2], ones1[:, :], cms[0:1, 5:7], start=True, stop=True))
                sv = grp("scalar", [(c_t, tv)], c_s,
                         lambda s, scl_col=scl_col: s.copy(scl[:, scl_col:scl_col + 2], ps[pi][:, 0:2]))
                ps_ev[pi] = sv
                return sv

            if os.environ.get("KPHASE") == "p0":
                dgrp("gpsimd", [(c_v, c_v.n), (c_s, c_s.n)], pool_wg,
                     [(xq_dm[0:128, 0:NG], stat_m[:, 0:NG]),
                      (xq_dm[0:128, NG:2 * NG], stat_r[:, 0:NG]),
                      (xq_dm[0:128, 2 * NG:2 * NG + 8], mx[:, 0:8])])
            ckpt("p0")
            P.phase = "c1"
            c1 = coll_max(0, 0, 0)
            if os.environ.get("KPHASE") == "c1":
                dgrp("gpsimd", [(c_v, c_v.n), (c_s, c_s.n), (c_t, c_t.n)], pool_wg,
                     [(up_sp[0:128, 0:16], scl[:, 0:16]),
                      (up_sp[0:128, 16:24], mx[:, 0:8]),
                      (up_sp[0:1, 24:40], cms[0:1, 0:16]),
                      (up_sp[0:1, 40:41], cc_i[0].rearrange("a b -> b a")),
                      (up_sp[0:1, 48:56], cc_o[0].rearrange("a b -> b a"))])
            ckpt("c1")

            P.phase = "f1"
            for g in range(NG):
                wx = dgrp("sync", [(c_v, c_v.n), (c_s, c_s.n)], pool_ld,
                          [(ta[:, 0:D], x[g * 128:(g + 1) * 128, :])])
                vv = grp("vector", [wx, (c_s, c1)], c_v, lambda v, g=g: (
                    v.tensor_scalar(tb[:, 0:D], ta[:, 0:D], stat_m[:, g:g + 1], stat_r[:, g:g + 1],
                                    ALU.subtract, ALU.mult),
                    v.tensor_scalar(tb[:, 0:D], tb[:, 0:D], scl[:, 1:2], None, ALU.mult))[-1])
                sv = grp("scalar", [(c_v, vv)], c_s, lambda s: s.add(tb[:, 0:D], tb[:, 0:D], MAGIC))
                sv = grp("scalar", [], c_s, lambda s: s.add(tb[:, 0:D], tb[:, 0:D], -MAGIC))
                vv = grp("vector", [(c_s, sv)], c_v, lambda v: (
                    v.tensor_scalar(tb[:, 0:D], tb[:, 0:D], -QMAX - 1.0, QMAX, ALU.max, ALU.min),
                    v.tensor_scalar(tb[:, 0:D], tb[:, 0:D], scl[:, 0:1], None, ALU.mult),
                    v.tensor_copy(xnb[:], tb[:, 0:D]))[-1])
                spill_T(xnb, xq_dm, g, vv)

            def lpbq(w_ap, O, I, lhsT_sp):
                NO = O // 128
                IH = I if I <= 2048 else I // 2
                NB = IH // 32
                STL = [pool_st.all(), pool_st.all()]
                for ot in range(NO):
                    for ih in range(I // IH):
                        wld = dgrp("sync", [(c_v, c_v.n), (c_s, c_s.n)], pool_ld,
                                   [(ta[:, 0:IH], w_ap[ot * 128:(ot + 1) * 128, ih * IH:(ih + 1) * IH])])
                        vv = grp("vector", [wld], c_v, lambda v, IH=IH, NB=NB: v.reduce_max(
                            tcp[:, 0:NB].rearrange("p (a b) -> p a b", b=1),
                            ta[:, 0:IH].rearrange("p (a b) -> p a b", b=32),
                            axis=AX, apply_absolute_value=True))
                        vv = grp("vector", [], c_v, lambda v, NB=NB: v.tensor_scalar(
                            tcp[:, NB:2 * NB], tcp[:, 0:NB], 1.0 / 7.0, None, ALU.mult))
                        vv = grp("vector", [], c_v, lambda v, NB=NB: v.tensor_scalar_max(
                            tcp[:, 0:NB], tcp[:, NB:2 * NB], 1e-12))
                        vv = grp("vector", [], c_v, lambda v, NB=NB: v.reciprocal(
                            tb[:, 0:NB], tcp[:, 0:NB]))
                        vv = grp("vector", [], c_v, lambda v, IH=IH, NB=NB: v.tensor_tensor(
                            ta[:, 0:IH].rearrange("p (a b) -> p a b", b=32),
                            ta[:, 0:IH].rearrange("p (a b) -> p a b", b=32),
                            tb[:, 0:NB].rearrange("p (a b) -> p a b", b=1).broadcast_to([128, NB, 32]),
                            ALU.mult))
                        sv = grp("scalar", [(c_v, vv)], c_s,
                                 lambda s, IH=IH: s.add(ta[:, 0:IH], ta[:, 0:IH], MAGIC))
                        sv = grp("scalar", [], c_s,
                                 lambda s, IH=IH: s.add(ta[:, 0:IH], ta[:, 0:IH], -MAGIC))
                        vv = grp("vector", [(c_s, sv)], c_v, lambda v, IH=IH, NB=NB: (
                            v.tensor_scalar(ta[:, 0:IH], ta[:, 0:IH], -8.0, 7.0, ALU.max, ALU.min),
                            v.tensor_tensor(ta[:, 0:IH].rearrange("p (a b) -> p a b", b=32),
                                            ta[:, 0:IH].rearrange("p (a b) -> p a b", b=32),
                                            tcp[:, 0:NB].rearrange("p (a b) -> p a b", b=1).broadcast_to([128, NB, 32]),
                                            ALU.mult),
                            v.tensor_copy(ubank[:, 0:IH], ta[:, 0:IH]))[-1])
                        its = IH // 128
                        par = (ot * (I // IH) + ih) % 2
                        off = 2048 + par * 1536
                        sv2 = None
                        for it in range(its):
                            pi = it % 2
                            tv = grp("tensor", [(c_v, vv), (c_s, psb_ev[pi])], c_t,
                                     lambda t, it=it, pi=pi: t.transpose(psb[pi][:, 0:128],
                                                                         ubank[:, it * 128:(it + 1) * 128], ident[:]))
                            sv2 = grp("scalar", [(c_t, tv), (STL[par] if it == 0 else [])], c_s,
                                      lambda s, it=it, pi=pi, off=off: s.copy(
                                          ustage[:, off + it * 128:off + (it + 1) * 128], psb[pi][:, 0:128]))
                            psb_ev[pi] = sv2
                        STL[par] = dgrp("sync", [(c_s, sv2)], pool_st,
                                        [(lhsT_sp[ih * its * 128:(ih + 1) * its * 128,
                                                  ot * 128:(ot + 1) * 128].rearrange("(a p) b -> p a b", p=128),
                                          ustage[:, off:off + its * 128].rearrange("p (a b) -> p a b", b=128))])
                return pool_st.all()

            def mm_pass(lhsT_sp, NK, NO, src_dm, dst_sp, mx_col, wready):
                OH = 12
                stflush = pool_st.all()
                Tmm = [c_t.n, c_t.n]
                STM = [[], []]
                first = [True, True]
                for oh in range((NO + OH - 1) // OH):
                    o0, o1 = oh * OH, min((oh + 1) * OH, NO)
                    no = o1 - o0
                    wld = dgrp("gpsimd", [wready, (c_t, c_t.n)], pool_wg,
                               [(wbig[:, (k * no) * 128:((k + 1) * no) * 128],
                                 lhsT_sp[k * 128:(k + 1) * 128, o0 * 128:o1 * 128]) for k in range(NK)])
                    for tch in range(NT5):
                        j = tch % 2
                        sb = j * NK
                        mv = dgrp("sync", [(c_t, Tmm[j]), (stflush if first[j] else [])], pool_ld,
                                  [(ubank[:, (sb + k) * 512:(sb + k + 1) * 512],
                                    src_dm[k * 128:(k + 1) * 128, tch * 512:(tch + 1) * 512]) for k in range(NK)])
                        first[j] = False
                        for o in range(o0, o1):
                            pi = o % 2
                            tv = grp("tensor", [mv, wld, (c_s, ps_ev[pi])], c_t,
                                     lambda t, o=o, pi=pi, no=no, o0=o0, NK=NK, sb=sb: [
                                         t.matmul(ps[pi][:, 0:512],
                                                  wbig[:, (k * no + o - o0) * 128:(k * no + o - o0 + 1) * 128],
                                                  ubank[:, (sb + k) * 512:(sb + k + 1) * 512],
                                                  start=(k == 0), stop=(k == NK - 1))
                                         for k in range(NK)][-1])
                            sv = grp("scalar", [(c_t, tv), STM[pi]], c_s,
                                     lambda s, pi=pi, o=o: s.copy(fa[:] if o % 2 == 0 else fb[:],
                                                                  ps[pi][:, 0:512]))
                            ps_ev[pi] = sv
                            vv = grp("vector", [(c_s, sv)], c_v, lambda v, o=o: v.reduce_max(
                                tcp[:, 2:3], (fa[:] if o % 2 == 0 else fb[:]), axis=AX,
                                apply_absolute_value=True))
                            vv = grp("vector", [], c_v, lambda v, mx_col=mx_col: v.tensor_max(
                                mx[:, mx_col:mx_col + 1], mx[:, mx_col:mx_col + 1], tcp[:, 2:3]))
                            STM[pi] = dgrp("sync", [(c_v, vv)], pool_st,
                                           [(dst_sp[o * 128:(o + 1) * 128, tch * 512:(tch + 1) * 512],
                                             fa[:] if o % 2 == 0 else fb[:])])
                        Tmm[j] = c_t.n
                return pool_st.all()

            if os.environ.get("KPHASE") == "f1":
                dgrp("gpsimd", [(c_v, c_v.n), (c_s, c_s.n)], pool_wg,
                     [(up_sp[0:128, 0:16], scl[:, 0:16]),
                      (up_sp[0:128, 16:24], mx[:, 0:8])])
            ckpt("f1")
            P.phase = "ffnmm"
            lu = lpbq(dram["wup"], I3, D, wqTu)
            mm_pass(wqTu, NDM, NI3, xq_dm, up_sp, 1, lu)
            lg = lpbq(dram["wgate"], I3, D, wqTg)
            mm_pass(wqTg, NDM, NI3, xq_dm, gate_sp, 2, lg)

            ckpt("ffnmm")
            P.phase = "c2"
            c2 = coll_max(1, 1, 2)
            c3 = coll_max(2, 2, 4)

            P.phase = "f4"
            V4 = [c_v.n, c_v.n]
            S4 = [c_s.n, c_s.n]
            ST4 = [pool_st.all(), pool_st.all()]
            for idx in range(NI3 * NT5):
                it, tch = divmod(idx, NT5)
                j = idx % 2
                fin = fa if j == 0 else fa2
                fout = fb if j == 0 else fb2
                taj = ta[:, j * 512:(j + 1) * 512]
                tbj = tb[:, j * 512:(j + 1) * 512]
                rc = tcp[:, 2 + j:3 + j]
                wld = dgrp("sync", [(c_s, c3), (c_v, V4[j]), (c_s, S4[j]), ST4[j]], pool_ld,
                           [(fin[:], gate_sp[it * 128:(it + 1) * 128, tch * 512:(tch + 1) * 512])])
                vv = grp("vector", [wld], c_v,
                         lambda v, fin=fin, taj=taj: v.tensor_scalar(taj, fin[:], scl[:, 5:6], None, ALU.mult))
                sv = grp("scalar", [(c_v, vv)], c_s, lambda s, taj=taj: s.add(taj, taj, MAGIC))
                sv = grp("scalar", [], c_s, lambda s, taj=taj: s.add(taj, taj, -MAGIC))
                vv = grp("vector", [(c_s, sv)], c_v, lambda v, taj=taj: (
                    v.tensor_scalar(taj, taj, -QMAX - 1.0, QMAX, ALU.max, ALU.min),
                    v.tensor_scalar(taj, taj, scl[:, 4:5], None, ALU.mult))[-1])
                sv = grp("scalar", [(c_v, vv)], c_s,
                         lambda s, taj=taj, tbj=tbj: s.activation(out=tbj, in_=taj, func=AF.Sigmoid))
                sv = grp("scalar", [], c_s, lambda s, tbj=tbj: s.activation(
                    out=tbj, in_=tbj, func=AF.Copy, scale=65536.0, bias=MAGIC))
                sv = grp("scalar", [], c_s, lambda s, tbj=tbj: s.add(tbj, tbj, -MAGIC))
                S4[j] = sv
                vv = grp("vector", [(c_s, sv)], c_v, lambda v, taj=taj, tbj=tbj, rc=rc: (
                    v.tensor_scalar(tbj, tbj, 0.0, 65535.0, ALU.max, ALU.min),
                    v.tensor_scalar(tbj, tbj, 1.0 / 65536.0, None, ALU.mult),
                    v.tensor_tensor(taj, taj, tbj, ALU.mult),
                    v.reduce_max(rc, taj, axis=AX, apply_absolute_value=True))[-1])
                vv = grp("vector", [ST4[j]], c_v, lambda v, taj=taj, rc=rc, fout=fout: (
                    v.tensor_max(mx[:, 3:4], mx[:, 3:4], rc),
                    v.tensor_copy(fout[:], taj))[-1])
                V4[j] = vv
                ST4[j] = dgrp("sync", [(c_v, vv)], pool_st,
                              [(g2_sp[it * 128:(it + 1) * 128, tch * 512:(tch + 1) * 512], fout[:])])

            ckpt("f4")
            c4 = coll_max(3, 3, 6)

            P.phase = "f5"
            V5 = [c_v.n, c_v.n]
            S5 = [c_s.n, c_s.n]
            ST5 = [pool_st.all(), pool_st.all()]
            for idx in range(NI3 * NT5):
                it, tch = divmod(idx, NT5)
                j = idx % 2
                fin = fa if j == 0 else fa2
                fin2 = fb if j == 0 else fb2
                taj = ta[:, j * 512:(j + 1) * 512]
                tbj = tb[:, j * 512:(j + 1) * 512]
                rc = tcp[:, 2 + j:3 + j]
                wld = dgrp("sync", [(c_s, c4), (c_v, V5[j]), (c_s, S5[j]), ST5[j]], pool_ld,
                           [(fin[:], g2_sp[it * 128:(it + 1) * 128, tch * 512:(tch + 1) * 512]),
                            (fin2[:], up_sp[it * 128:(it + 1) * 128, tch * 512:(tch + 1) * 512])])
                vv = grp("vector", [wld], c_v, lambda v, fin=fin, fin2=fin2, taj=taj, tbj=tbj: (
                    v.tensor_scalar(taj, fin[:], scl[:, 7:8], None, ALU.mult),
                    v.tensor_scalar(tbj, fin2[:], scl[:, 3:4], None, ALU.mult))[-1])
                sv = grp("scalar", [(c_v, vv)], c_s, lambda s, taj=taj, tbj=tbj: (
                    s.add(taj, taj, MAGIC),
                    s.add(tbj, tbj, MAGIC))[-1])
                sv = grp("scalar", [], c_s, lambda s, taj=taj, tbj=tbj: (
                    s.add(taj, taj, -MAGIC),
                    s.add(tbj, tbj, -MAGIC))[-1])
                S5[j] = sv
                vv = grp("vector", [(c_s, sv)], c_v, lambda v, taj=taj, tbj=tbj, rc=rc: (
                    v.tensor_scalar(taj, taj, -QMAX - 1.0, QMAX, ALU.max, ALU.min),
                    v.tensor_scalar(tbj, tbj, -QMAX - 1.0, QMAX, ALU.max, ALU.min),
                    v.tensor_scalar(taj, taj, scl[:, 6:7], None, ALU.mult),
                    v.tensor_scalar(tbj, tbj, scl[:, 2:3], None, ALU.mult),
                    v.tensor_tensor(taj, taj, tbj, ALU.mult),
                    v.reduce_max(rc, taj, axis=AX, apply_absolute_value=True))[-1])
                vv = grp("vector", [], c_v, lambda v, taj=taj, rc=rc, fin=fin: (
                    v.tensor_max(mx[:, 4:5], mx[:, 4:5], rc),
                    v.tensor_copy(fin[:], taj))[-1])
                V5[j] = vv
                ST5[j] = dgrp("sync", [(c_v, vv)], pool_st,
                              [(o_sp[it * 128:(it + 1) * 128, tch * 512:(tch + 1) * 512], fin[:])])

            ckpt("f5")
            c5 = coll_max(4, 4, 8)

            P.phase = "f6"
            ld_ = lpbq(dram["wdown"], D, I3, wqTd)

            def ostage(k):
                return gat[:, k * 512:(k + 1) * 512] if k < 8 else ubank[:, (k - 8) * 512:(k - 7) * 512]

            OH = 4
            stf6 = pool_st.all()
            STF = [[], []]
            for oh in range(NDM // OH):
                o0 = oh * OH
                wld = dgrp("gpsimd", [ld_, (c_t, c_t.n)], pool_wg,
                           [(wbig[:, (k * OH) * 128:((k + 1) * OH) * 128],
                             wqTd[k * 128:(k + 1) * 128, o0 * 128:(o0 + OH) * 128]) for k in range(NI3)])
                for tch in range(NT5):
                    tq = c_t.n
                    V6 = [stf6 + [(c_v, c_v.n)], stf6 + [(c_v, c_v.n)]]
                    vq = 0
                    for k in range(NI3):
                        j = k % 2
                        fin = fa if j == 0 else fa2
                        taj = ta[:, j * 512:(j + 1) * 512]
                        wl = dgrp("sync", [(c_s, c5), (c_t, tq), V6[j]], pool_ld,
                                  [(fin[:], o_sp[k * 128:(k + 1) * 128, tch * 512:(tch + 1) * 512])])
                        vv = grp("vector", [wl], c_v,
                                 lambda v, fin=fin, taj=taj: v.tensor_scalar(taj, fin[:], scl[:, 9:10], None, ALU.mult))
                        sv = grp("scalar", [(c_v, vv)], c_s,
                                 lambda s, taj=taj: s.add(taj, taj, MAGIC))
                        sv = grp("scalar", [], c_s,
                                 lambda s, taj=taj: s.add(taj, taj, -MAGIC))
                        vq = grp("vector", [(c_s, sv)], c_v, lambda v, k=k, taj=taj: (
                            v.tensor_scalar(taj, taj, -QMAX - 1.0, QMAX, ALU.max, ALU.min),
                            v.tensor_scalar(taj, taj, scl[:, 8:9], None, ALU.mult),
                            v.tensor_copy(ostage(k), taj))[-1])
                        V6[j] = [(c_v, vq)]
                    for o in range(o0, o0 + OH):
                        pi = o % 2
                        fout6 = fo if o % 2 == 0 else xe[0]
                        tv = grp("tensor", [(c_v, vq), wld, (c_s, ps_ev[pi])], c_t,
                                 lambda t, o=o, pi=pi, o0=o0: [
                                     t.matmul(ps[pi][:, 0:512],
                                              wbig[:, (k * OH + o - o0) * 128:(k * OH + o - o0 + 1) * 128],
                                              ostage(k), start=(k == 0), stop=(k == NI3 - 1))
                                     for k in range(NI3)][-1])
                        sv = grp("scalar", [(c_t, tv), STF[pi]], c_s,
                                 lambda s, pi=pi, fout6=fout6: s.copy(fout6[:, 0:512], ps[pi][:, 0:512]))
                        ps_ev[pi] = sv
                        STF[pi] = dgrp("sync", [(c_s, sv)], pool_st,
                                       [(ffn_sp[o * 128:(o + 1) * 128, tch * 512:(tch + 1) * 512], fout6[:, 0:512])])

            def layer(l, src_dm, dst_dm):
                vz = grp("vector", [pool_ld.all(), pool_st.all()], c_v, lambda v: (
                    v.memset(hstate[:], 0.0),
                    v.memset(ccarry[:], 0.0))[-1])
                prevstore = []
                prevxdbl_t = 0
                for part in range(2):
                    wld = dgrp("gpsimd", [(c_t, c_t.n)], pool_wg,
                               [(wbig[:, (k * NDT) * 128:((k + 1) * NDT) * 128],
                                 dram[f"WinT{l}"][k * 128:(k + 1) * 128, part * DI:(part + 1) * DI])
                                for k in range(NDM)])
                    if part == 0:
                        wldx = dgrp("gpsimd", [(c_t, c_t.n)], pool_wg,
                                    [(wx_s[:, kt * 96:(kt + 1) * 96],
                                      dram[f"WxT{l}"][kt * 128:(kt + 1) * 128, :]) for kt in range(NDT)])
                    for tch in range(NT5):
                        mv = dgrp("sync", [(c_t, c_t.n), (pool_st.all() if tch == 0 and part == 0 else [])],
                                  pool_ld,
                                  [(ubank[:, k * 512:(k + 1) * 512],
                                    src_dm[k * 128:(k + 1) * 128, tch * 512:(tch + 1) * 512]) for k in range(NDM)])
                        uvv = vz
                        for o in range(NDT):
                            pi = o % 2
                            tv = grp("tensor", [mv, wld, (c_s, ps_ev[pi])], c_t,
                                     lambda t, o=o, pi=pi: [
                                         t.matmul(ps[pi][:, 0:512],
                                                  wbig[:, (k * NDT + o) * 128:(k * NDT + o + 1) * 128],
                                                  ubank[:, k * 512:(k + 1) * 512],
                                                  start=(k == 0), stop=(k == NDM - 1))
                                         for k in range(NDM)][-1])
                            if part == 0:
                                sv = grp("scalar", [(c_t, tv), (c_v, uvv),
                                                    prevstore, (c_t, prevxdbl_t)], c_s,
                                         lambda s, pi=pi: s.copy(uext[:, 3:515], ps[pi][:, 0:512]))
                                ps_ev[pi] = sv
                                cwc = l * NDT * DC + o * DC
                                uvv = grp("vector", [(c_s, sv)], c_v, lambda v, o=o, cwc=cwc: (
                                    v.tensor_copy(uext[:, 0:3], ccarry[:, o * 3:o * 3 + 3]),
                                    v.tensor_copy(ccarry[:, o * 3:o * 3 + 3], uext[:, 512:515]),
                                    v.tensor_scalar(ub[:], uext[:, 0:512], convw[:, cwc + 0:cwc + 1], None, ALU.mult),
                                    v.scalar_tensor_tensor(ub[:], uext[:, 1:513], convw[:, cwc + 1:cwc + 2], ub[:],
                                                           ALU.mult, ALU.add),
                                    v.scalar_tensor_tensor(ub[:], uext[:, 2:514], convw[:, cwc + 2:cwc + 3], ub[:],
                                                           ALU.mult, ALU.add),
                                    v.scalar_tensor_tensor(ub[:], uext[:, 3:515], convw[:, cwc + 3:cwc + 4], ub[:],
                                                           ALU.mult, ALU.add))[-1])
                                grp("scalar", [(c_v, uvv)], c_s,
                                    lambda s, o=o, l=l: s.activation(
                                        out=ustage[:, o * 512:(o + 1) * 512], in_=ub[:], func=AF.Silu,
                                        bias=convb[:, l * NDT + o:l * NDT + o + 1]))
                            else:
                                sv = grp("scalar", [(c_t, tv), prevstore], c_s,
                                         lambda s, pi=pi, o=o: s.activation(
                                             out=ustage[:, o * 512:(o + 1) * 512], in_=ps[pi][:, 0:512],
                                             func=AF.Silu))
                                ps_ev[pi] = sv
                        stv = c_s.n
                        dst_sp = us_sp if part == 0 else sz_sp
                        prevstore = dgrp("sync", [(c_s, stv)], pool_st,
                                         [(dst_sp.rearrange("(a p) t -> p a t", p=128)[:, :, tch * 512:(tch + 1) * 512],
                                           ustage[:].rearrange("p (a t) -> p a t", a=NDT))])
                        if part == 0:
                            pi = 2
                            tv = grp("tensor", [(c_s, stv), (c_s, ps_ev[pi]), wldx], c_t,
                                     lambda t, pi=pi: [
                                         t.matmul(ps[pi][0:96, 0:512],
                                                  wx_s[:, kt * 96:(kt + 1) * 96],
                                                  ustage[:, kt * 512:(kt + 1) * 512],
                                                  start=(kt == 0), stop=(kt == NDT - 1))
                                         for kt in range(NDT)][-1])
                            prevxdbl_t = tv
                            sv = grp("scalar", [(c_t, tv)], c_s,
                                     lambda s, tch=tch, pi=pi: s.copy(xdbl_b[0:96, tch * 512:(tch + 1) * 512],
                                                                      ps[pi][0:96, 0:512]))
                            ps_ev[pi] = sv
                wldo = dgrp("gpsimd", [(c_t, c_t.n)], pool_wg,
                            [(wbig[:, (k * NDM) * 128:((k + 1) * NDM) * 128],
                              dram[f"WoutT{l}"][k * 128:(k + 1) * 128, :]) for k in range(NDT)])
                wldd = dgrp("gpsimd", [(c_t, c_t.n)], pool_wg,
                            [(wdt_s[0:64, dt * 128:(dt + 1) * 128],
                              dram[f"WdtT{l}"][:, (DI // 2) * 0 + dt * 128:dt * 128 + 128]) for dt in range(NSC)])
                pa_done_st = pool_st.all()
                prev_scan_v = 0
                prev_gx = []
                da_rd = [0, 0]
                dtb_v = {}
                STW = [pool_st.all(), pool_st.all()]
                pend = None

                def emit_wout(gv, c):
                    cs = c * TC
                    for sub in range(TC // 512):
                        mvw = dgrp("sync", [(c_g, gv), (c_t, c_t.n)], pool_ld,
                                   [(ubank[:, k * 512:(k + 1) * 512],
                                     gx_o[c % 2][k * 128:(k + 1) * 128, sub * 512:(sub + 1) * 512])
                                    for k in range(NDT)])
                        for o in range(NDM):
                            pi = o % 2
                            fw = fo if o % 2 == 0 else xe[0]
                            tv = grp("tensor", [mvw, wldo, (c_s, ps_ev[pi])], c_t,
                                     lambda t, o=o, pi=pi: [
                                         t.matmul(ps[pi][:, 0:512],
                                                  wbig[:, (k * NDM + o) * 128:(k * NDM + o + 1) * 128],
                                                  ubank[:, k * 512:(k + 1) * 512],
                                                  start=(k == 0), stop=(k == NDT - 1))
                                         for k in range(NDT)][-1])
                            sv = grp("scalar", [(c_t, tv), STW[pi]], c_s,
                                     lambda s, pi=pi, fw=fw: s.copy(fw[:, 0:512], ps[pi][:, 0:512]))
                            ps_ev[pi] = sv
                            STW[pi] = dgrp("sync", [(c_s, sv)], pool_st,
                                           [(dst_dm[o * 128:(o + 1) * 128, cs + sub * 512:cs + (sub + 1) * 512],
                                             fw[:, 0:512])])

                for c in range(NTC):
                    cs = c * TC
                    for n in range(DS):
                        for (jsel, dstt) in ((n, bcB), (16 + n, bcC)):
                            pi = 2 + (n % 2)
                            tv = grp("tensor", [(c_s, c_s.n), (c_s, ps_ev[pi]), wsel], c_t,
                                     lambda t, jsel=jsel, pi=pi, cs=cs: t.matmul(
                                         ps[pi][:, 0:TC], selb[64:96, jsel * 128:(jsel + 1) * 128],
                                         xdbl_b[64:96, cs:cs + TC], start=True, stop=True))
                            sv = grp("scalar", [(c_t, tv), (c_v, prev_scan_v)], c_s,
                                     lambda s, n=n, dstt=dstt, pi=pi: s.copy(dstt[:, n * TC:(n + 1) * TC],
                                                                             ps[pi][:, 0:TC]))
                            ps_ev[pi] = sv
                    bc_s = c_s.n
                    for dt in range(NSC):
                        mvu = dgrp("sync", [(c_v, prev_scan_v), (pa_done_st if c == 0 else None)], pool_ld,
                                   [(uball[:, dt * TC:(dt + 1) * TC], us_sp[dt * 128:(dt + 1) * 128, cs:cs + TC]),
                                    (zball[:, dt * TC:(dt + 1) * TC], sz_sp[dt * 128:(dt + 1) * 128, cs:cs + TC])])
                        pi = 2
                        tv = grp("tensor", [(c_s, ps_ev[pi]), wldd], c_t,
                                 lambda t, dt=dt, pi=pi, cs=cs: t.matmul(
                                     ps[pi][:, 0:TC], wdt_s[0:64, dt * 128:(dt + 1) * 128],
                                     xdbl_b[0:64, cs:cs + TC], start=True, stop=True))
                        sv = grp("scalar", [(c_t, tv), (c_v, prev_scan_v),
                                            (c_v, dtb_v.get(dt - 1, 0))], c_s,
                                 lambda s, dt=dt, pi=pi, l=l: s.activation(
                                     out=ta[:, 0:TC], in_=ps[pi][:, 0:TC],
                                     func=AF.Identity, bias=bdt[:, l * NSC + dt:l * NSC + dt + 1]))
                        ps_ev[pi] = sv
                        sv = grp("scalar", [(c_s, sv)], c_s,
                                 lambda s: s.activation(out=tb[:, 0:TC], in_=ta[:, 0:TC], func=AF.Abs))
                        sv = grp("scalar", [], c_s,
                                 lambda s: s.activation(out=tb[:, 0:TC], in_=tb[:, 0:TC], func=AF.Exp, scale=-1.0))
                        sv = grp("scalar", [], c_s,
                                 lambda s: s.activation(out=tb[:, 0:TC], in_=tb[:, 0:TC], func=AF.Ln, bias=1.0))
                        dtb_v[dt] = grp("vector", [(c_s, sv), mvu], c_v, lambda v, dt=dt: (
                            v.tensor_scalar_max(ta[:, 0:TC], ta[:, 0:TC], 0.0),
                            v.tensor_tensor(dtball[:, dt * TC:(dt + 1) * TC], ta[:, 0:TC], tb[:, 0:TC], ALU.add),
                            v.tensor_tensor(dtuall[:, dt * TC:(dt + 1) * TC],
                                            dtball[:, dt * TC:(dt + 1) * TC],
                                            uball[:, dt * TC:(dt + 1) * TC], ALU.mult))[-1])
                    for dt in range(NSC):
                        for n in range(DS):
                            pp = (dt * DS + n) % 2
                            sv = grp("scalar", [(c_s, bc_s), (c_v, da_rd[pp]), (c_v, dtb_v[dt])], c_s,
                                     lambda s, dt=dt, n=n, pp=pp, l=l: s.activation(
                                         out=da[pp][:], in_=dtball[:, dt * TC:(dt + 1) * TC], func=AF.Exp,
                                         scale=Aneg[:, l * NSC * DS + dt * DS + n:l * NSC * DS + dt * DS + n + 1]))
                            hs = dt * DS + n
                            if os.environ.get("KDBG") and l == 0 and c == 0 and dt == 0 and n == 0:
                                dbg_v = grp("vector", [(c_s, sv)], c_v, lambda v: (
                                    v.tensor_tensor(dbu[:], dtuall[:, 0:TC], bcB[:, 0:TC], ALU.mult),
                                    v.tensor_copy(ta[:, 0:TC], da[pp][:]),
                                    v.tensor_copy(tb[:, 0:TC], dbu[:]),
                                    v.tensor_copy(tcp[:, 0:TC // 2], dtball[:, 0:TC // 2]))[-1])
                                st1 = dgrp("sync", [(c_v, dbg_v)], pool_st, [
                                    (out[0:128, 0:TC], ta[:, 0:TC]),
                                    (out[128:256, 0:TC], tb[:, 0:TC]),
                                    (out[256:384, 0:TC // 2], tcp[:, 0:TC // 2])])
                                dbg_v2 = grp("vector", [(c_v, dbg_v), st1], c_v, lambda v: (
                                    v.tensor_copy(ta[:, 0:TC], bcB[:, 0:TC]),
                                    v.tensor_copy(tb[:, 0:TC], uball[:, 0:TC]),
                                    v.tensor_copy(tcp[0:96, 0:512], xdbl_b[0:96, 0:512]))[-1])
                                dgrp("sync", [(c_v, dbg_v2)], pool_st, [
                                    (out[384:512, 0:TC], ta[:, 0:TC]),
                                    (out[256:384, TC // 2:TC // 2 + TC] if False else out[128 + 256:128 + 384, 0:0 + TC], tb[:, 0:TC])])
                                raise StopIteration
                            da_rd[pp] = grp("vector", [(c_s, sv)], c_v, lambda v, dt=dt, n=n, pp=pp, hs=hs: (
                                v.tensor_tensor(dbu[:], dtuall[:, dt * TC:(dt + 1) * TC],
                                                bcB[:, n * TC:(n + 1) * TC], ALU.mult),
                                v.tensor_tensor_scan(hb[:], da[pp][:], dbu[:],
                                                     hstate[:, hs:hs + 1], ALU.mult, ALU.add),
                                v.tensor_copy(hstate[:, hs:hs + 1], hb[:, TC - 1:TC]),
                                v.tensor_tensor(hcb[:], hb[:], bcC[:, n * TC:(n + 1) * TC], ALU.mult),
                                (v.tensor_copy(yball[:, dt * TC:(dt + 1) * TC], hcb[:]) if n == 0 else
                                 v.tensor_tensor(yball[:, dt * TC:(dt + 1) * TC],
                                                 yball[:, dt * TC:(dt + 1) * TC], hcb[:], ALU.add)))[-1])
                        grp("vector", [prev_gx], c_v, lambda v, dt=dt, l=l: (
                            v.scalar_tensor_tensor(yball[:, dt * TC:(dt + 1) * TC],
                                                   uball[:, dt * TC:(dt + 1) * TC],
                                                   Dvt[:, l * NSC + dt:l * NSC + dt + 1],
                                                   yball[:, dt * TC:(dt + 1) * TC], ALU.mult, ALU.add),
                            v.tensor_tensor(gat[:, dt * TC:(dt + 1) * TC], yball[:, dt * TC:(dt + 1) * TC],
                                            zball[:, dt * TC:(dt + 1) * TC], ALU.mult))[-1])
                        if dt == 2 and pend is not None:
                            emit_wout(*pend)
                            pend = None
                    prev_scan_v = c_v.n
                    prev_gx = dgrp("sync", [(c_v, prev_scan_v), (c_g, c_g.n)], pool_st,
                                   [(gx_i[c % 2][:].rearrange("(a p) t -> p a t", p=128),
                                     gat[:].rearrange("p (a t) -> p a t", a=NSC))])
                    gv = grp("gpsimd", [prev_gx], c_g, lambda g, c=c: g.collective_compute(
                        "AllGather", ALU.bypass, ins=[gx_i[c % 2][:]], outs=[gx_o[c % 2][:]],
                        replica_groups=PAIRS))
                    pend = (gv, c)
                if pend is not None:
                    emit_wout(*pend)
                    pend = None

            P.phase = "layers"
            ckpt("f6")
            try:
                layer(0, xn_dm, hmid_sp)
            except StopIteration:
                pass
            ckpt("L0")
            try:
                layer(1, hmid_sp, h2_sp)
            except StopIteration:
                pass
            ckpt("L1")

            P.phase = "asm"
            SKIP_ASM = bool(os.environ.get("KDBG"))
            outst = []
            for tch in range([] if SKIP_ASM else range(NT5)) if False else (range(0) if SKIP_ASM else range(NT5)):
                mv = dgrp("sync", [(c_v, c_v.n), (pool_st.all() if tch == 0 else [])], pool_ld,
                          [(uball[:, k * 512:(k + 1) * 512],
                            h2_sp[k * 128:(k + 1) * 128, tch * 512:(tch + 1) * 512]) for k in range(NDM)] +
                          [(zball[:, k * 512:(k + 1) * 512],
                            ffn_sp[k * 128:(k + 1) * 128, tch * 512:(tch + 1) * 512]) for k in range(NDM)])
                vv = grp("vector", [mv], c_v, lambda v: [
                    v.tensor_tensor(yball[:, k * 512:(k + 1) * 512], uball[:, k * 512:(k + 1) * 512],
                                    zball[:, k * 512:(k + 1) * 512], ALU.add) for k in range(NDM)][-1])
                for g4 in range(4):
                    g = tch * 4 + g4
                    for o in range(NDM):
                        pi = 2 + (o % 2)
                        tv = grp("tensor", [(c_v, vv), (c_s, ps_ev[pi])], c_t,
                                 lambda t, o=o, g4=g4, pi=pi: t.transpose(
                                     ps[pi][:, 0:128], yball[:, o * 512 + g4 * 128:o * 512 + (g4 + 1) * 128],
                                     identf[:]))
                        sv = grp("scalar", [(c_t, tv), outst], c_s,
                                 lambda s, o=o, pi=pi: s.copy(tb[:, o * 128:(o + 1) * 128], ps[pi][:, 0:128]))
                        ps_ev[pi] = sv
                    wx = dgrp("sync", [(c_v, c_v.n)], pool_ld, [(ta[:, 0:D], x[g * 128:(g + 1) * 128, :])])
                    vv2 = grp("vector", [wx, (c_s, c_s.n)], c_v, lambda v, g=g: (
                        v.tensor_scalar(tcp[:, 0:D], ta[:, 0:D],
                                        stat_m[:, g:g + 1], stat_r[:, g:g + 1], ALU.subtract, ALU.mult),
                        v.tensor_tensor(tb[:, 0:D], tb[:, 0:D], tcp[:, 0:D], ALU.add),
                        v.tensor_tensor(tb[:, 0:D], tb[:, 0:D], ta[:, 0:D], ALU.add))[-1])
                    outst = dgrp("sync", [(c_v, vv2)], pool_st, [(out[g * 128:(g + 1) * 128, :], tb[:, 0:D])])

        try:
            emit_all()
        except StopIteration:
            pass

        fin_deps = pool_ld.all() + pool_st.all() + pool_wg.all()

        def f_fin(e, fin_deps=fin_deps):
            for (c, v) in fin_deps:
                e.wait_ge(c.sem, v)
        P.on("gpsimd", f_fin)

        with nc.Block() as blk:
            @blk.sync
            def _(sy):
                P.emit("sync", sy, nc)

            @blk.vector
            def _(v):
                P.emit("vector", v, nc)

            @blk.scalar
            def _(s):
                P.emit("scalar", s, nc)

            @blk.gpsimd
            def _(g):
                P.emit("gpsimd", g, nc)

            @blk.tensor
            def _(t):
                P.emit("tensor", t, nc)

    return dram


def make_in_maps(inputs, T=4096):
    xf = np.ascontiguousarray(inputs["L0_attn_x"])
    ident = np.eye(128, dtype=np.float32)
    kconst = np.tile(np.array([1e-5, -0.5, MAGIC, -MAGIC, 65536.0, 1.0, 0, 0], np.float32), (128, 1))
    selbc = np.zeros((128, 32 * 128), np.float32)
    for jj in range(32):
        selbc[64 + jj, jj * 128:(jj + 1) * 128] = 1.0
    in_maps = []
    for c in range(N_CORES):
        b, j = c // 2, c % 2
        perm = np.arange(DI) if j == 0 else np.r_[DI // 2:DI, 0:DI // 2]
        im = {"x": np.ascontiguousarray(xf[b][:T]), "ident": ident, "kconst": kconst, "selbc": selbc}
        for l in range(2):
            Wi = inputs["L0_attn_W_in"][l]
            Wu, Wz = Wi[:DI][perm], Wi[DI:][perm]
            im[f"WinT{l}"] = np.ascontiguousarray(np.concatenate([Wu, Wz], 0).T)
            im[f"WxT{l}"] = np.ascontiguousarray(inputs["L0_attn_W_x"][l][:, perm].T)
            im[f"WdtT{l}"] = np.ascontiguousarray(inputs["L0_attn_W_dt"][l][perm].T)
            im[f"WoutT{l}"] = np.ascontiguousarray(inputs["L0_attn_W_out"][l].T)
            im[f"convw{l}"] = np.ascontiguousarray(inputs["L0_attn_conv_w"][l][perm])
            im[f"convb{l}"] = np.ascontiguousarray(inputs["L0_attn_conv_b"][l][perm][:, None])
            im[f"bdt{l}"] = np.ascontiguousarray(inputs["L0_attn_b_dt"][l][perm][:, None])
            im[f"Alog{l}"] = np.ascontiguousarray(inputs["L0_attn_A_log"][l][perm])
            im[f"Dv{l}"] = np.ascontiguousarray(inputs["L0_attn_D"][l][perm][:, None])
        im["wup"] = np.ascontiguousarray(inputs["L0_ffn_w_up"])
        im["wgate"] = np.ascontiguousarray(inputs["L0_ffn_w_gate"])
        im["wdown"] = np.ascontiguousarray(inputs["L0_ffn_w_down"])
        in_maps.append(im)
    return in_maps


def kernel(**inputs):
    T = 4096
    nc = bass.Bass("TRN2", target_bir_lowering=False, detect_race_conditions=False)
    build(nc, T)
    in_maps = make_in_maps(inputs, T)
    res = run_bass_kernel_spmd(nc, in_maps, core_ids=list(range(N_CORES)))
    out = np.stack([res.results[2 * b]["out"] for b in range(4)], 0)
    return out.astype(np.float32)
